# version 39
# baseline (speedup 1.0000x reference)
import numpy as np
from contextlib import ExitStack

import concourse.bass as bass
import concourse.mybir as mybir
from concourse.bass_utils import run_bass_kernel_spmd

F32 = mybir.dt.float32
BF16 = mybir.dt.bfloat16
AF = mybir.ActivationFunctionType
ALU = mybir.AluOpType
AX = mybir.AxisListType

D, DC = 2048, 16
DFF, FC = 5504, 43
T, NT = 1024, 8
TH, THW = 2, 512
H = 16
E, R, PK = 8, 8, 72
NCORES = 8

SKIP_FFN = False
SKIP_ATTN = False
DEBUG_DUMP = False
LAST_RES = None


class Ctr:
    def __init__(self, sem):
        self.sem = sem
        self.n = 0

    def add(self, k=1):
        self.n += k
        return self.n


def _prep(inp):
    g = {k: np.ascontiguousarray(np.asarray(v, dtype=np.float32)) for k, v in inp.items()}
    out = {}

    x = g["L0_attn_x"]
    xs = [np.ascontiguousarray(x[c // 4, (c % 4) * T:(c % 4 + 1) * T, :]) for c in range(NCORES)]

    def cols_from_wT(wT, bias):
        M = wT.shape[1]
        mb = M // 128
        a = wT.reshape(DC, 128, mb, 128).transpose(2, 1, 0, 3)
        bch = np.zeros((mb, 128, 1, 128), np.float32)
        bch[:, 0, 0, :] = bias.reshape(mb, 128)
        return np.ascontiguousarray(
            np.concatenate([a, bch], axis=2).reshape(mb * 128, 17 * 128))

    out["wq_cols"] = cols_from_wT(g["L0_attn_wq"].T, g["L0_attn_bq"])
    out["wo_cols"] = cols_from_wT(g["L0_attn_wo"].T, g["L0_attn_bo"])

    def kvx(w, bias):
        pad = np.zeros((128, D), np.float32)
        pad[0, :] = bias
        return np.ascontiguousarray(np.concatenate([w.T, pad], axis=0))

    def kvtile(a):
        return np.ascontiguousarray(
            a.reshape(17, 128, 8, 256).transpose(2, 1, 0, 3).reshape(8 * 128, 17 * 256))

    out["wkx"] = kvtile(kvx(g["L0_attn_wk"], g["L0_attn_bk"]))
    out["wvx"] = kvtile(kvx(g["L0_attn_wv"], g["L0_attn_bv"]))

    def gu_cols(W, Bflat, sB):
        a = W.reshape(DC, 128, FC, 128).transpose(2, 1, 0, 3)
        comb = np.zeros((FC, 128, 1, 128), np.float32)
        comb[:, 0:64, 0, :] = Bflat.reshape(64, FC, 128).transpose(1, 0, 2)
        comb[:, 64:72, 0, :] = sB.reshape(R, FC, 128).transpose(1, 0, 2)
        return np.ascontiguousarray(
            np.concatenate([a, comb], axis=2).reshape(FC * 128, 17 * 128))

    out["g_cols"] = gu_cols(g["L0_ffn_W_gate"], g["L0_ffn_B_gate"].reshape(64, DFF),
                            g["L0_ffn_sB_gate"])
    out["u_cols"] = gu_cols(g["L0_ffn_W_up"], g["L0_ffn_B_up"].reshape(64, DFF),
                            g["L0_ffn_sB_up"])

    a = g["L0_ffn_W_down"].reshape(FC, 128, DC, 128).transpose(2, 1, 0, 3)
    comb = np.zeros((DC, 128, 1, 128), np.float32)
    comb[:, 0:64, 0, :] = g["L0_ffn_B_down"].reshape(64, DC, 128).transpose(1, 0, 2)
    comb[:, 64:72, 0, :] = g["L0_ffn_sB_down"].reshape(R, DC, 128).transpose(1, 0, 2)
    out["d_cols"] = np.ascontiguousarray(
        np.concatenate([a, comb], axis=2).reshape(DC * 128, 44 * 128))

    def pack(Aexp, sA, n, nb):
        p = np.concatenate([Aexp.transpose(1, 0, 2).reshape(n, 64), sA], axis=1)
        return np.ascontiguousarray(
            p.reshape(nb, 128, PK).transpose(1, 0, 2).reshape(128, nb * PK))

    out["gpack"] = pack(g["L0_ffn_A_gate"], g["L0_ffn_sA_gate"], D, DC)
    out["upack"] = pack(g["L0_ffn_A_up"], g["L0_ffn_sA_up"], D, DC)
    out["dpack"] = pack(g["L0_ffn_A_down"], g["L0_ffn_sA_down"], DFF, FC)

    out["wr_t"] = np.ascontiguousarray(
        g["L0_ffn_W_router"].reshape(DC, 128, E).transpose(1, 0, 2).reshape(128, DC * E))

    out["ident"] = np.eye(128, dtype=np.float32)
    onesx = np.zeros((128, THW), np.float32)
    onesx[0, :] = 1.0
    out["onesx"] = onesx
    pat = np.zeros((E, 64), np.float32)
    for e in range(E):
        pat[e, e * 8:(e + 1) * 8] = 1.0
    out["pattern"] = pat
    out["ebias_b"] = np.ascontiguousarray(np.tile(g["L0_ffn_expert_bias"][None, :], (128, 1)))
    cst = np.zeros((128, 8), np.float32)
    cst[:, 0] = 1e-8
    cst[:, 1] = 1e-5
    cst[:, 2] = -1.0
    cst[:, 3] = -0.5
    out["consts"] = cst
    return xs, out


def build_nc():
    nc = bass.Bass("TRN2", target_bir_lowering=False)

    x_h = nc.dram_tensor("x", [T, D], F32, kind="ExternalInput").ap()
    wq_h = nc.dram_tensor("wq_cols", [DC * 128, 17 * 128], F32, kind="ExternalInput").ap()
    wo_h = nc.dram_tensor("wo_cols", [DC * 128, 17 * 128], F32, kind="ExternalInput").ap()
    wkx_h = nc.dram_tensor("wkx", [8 * 128, 17 * 256], F32, kind="ExternalInput").ap()
    wvx_h = nc.dram_tensor("wvx", [8 * 128, 17 * 256], F32, kind="ExternalInput").ap()
    gcols_h = nc.dram_tensor("g_cols", [FC * 128, 17 * 128], F32, kind="ExternalInput").ap()
    ucols_h = nc.dram_tensor("u_cols", [FC * 128, 17 * 128], F32, kind="ExternalInput").ap()
    dcols_h = nc.dram_tensor("d_cols", [DC * 128, 44 * 128], F32, kind="ExternalInput").ap()
    gpack_h = nc.dram_tensor("gpack", [128, DC * PK], F32, kind="ExternalInput").ap()
    upack_h = nc.dram_tensor("upack", [128, DC * PK], F32, kind="ExternalInput").ap()
    dpack_h = nc.dram_tensor("dpack", [128, FC * PK], F32, kind="ExternalInput").ap()
    wr_h = nc.dram_tensor("wr_t", [128, DC * E], F32, kind="ExternalInput").ap()
    id_h = nc.dram_tensor("ident", [128, 128], F32, kind="ExternalInput").ap()
    onesx_h = nc.dram_tensor("onesx", [128, THW], F32, kind="ExternalInput").ap()
    pat_h = nc.dram_tensor("pattern", [E, 64], F32, kind="ExternalInput").ap()
    ebias_h = nc.dram_tensor("ebias_b", [128, E], F32, kind="ExternalInput").ap()
    cst_h = nc.dram_tensor("consts", [128, 8], F32, kind="ExternalInput").ap()
    out_h = nc.dram_tensor("out", [T, D], F32, kind="ExternalOutput").ap()

    dbg = {}
    if DEBUG_DUMP:
        dbg["d_xn"] = nc.dram_tensor("d_xn", [128, DC * T], BF16, kind="ExternalOutput").ap()
        dbg["d_ktm"] = nc.dram_tensor("d_ktm", [128, NT * D], BF16, kind="ExternalOutput").ap()
        dbg["d_vaug"] = nc.dram_tensor("d_vaug", [128, NT * H * 129], BF16, kind="ExternalOutput").ap()
        dbg["d_scores"] = nc.dram_tensor("d_scores", [128, NT * E], F32, kind="ExternalOutput").ap()
        dbg["d_wtm"] = nc.dram_tensor("d_wtm", [128, NT * E], F32, kind="ExternalOutput").ap()
        dbg["d_wb"] = nc.dram_tensor("d_wb", [64, T], F32, kind="ExternalOutput").ap()
        dbg["d_kvp"] = nc.dram_tensor("d_kvp", [128, H * 129], F32, kind="ExternalOutput").ap()
        dbg["d_kvbf"] = nc.dram_tensor("d_kvbf", [128, H * 129], BF16, kind="ExternalOutput").ap()
        dbg["d_qfm"] = nc.dram_tensor("d_qfm", [128, H * T], BF16, kind="ExternalOutput").ap()
        dbg["d_attn"] = nc.dram_tensor("d_attn", [128, H * T], BF16, kind="ExternalOutput").ap()
        dbg["d_ffn"] = nc.dram_tensor("d_ffn", [D, T], F32, kind="ExternalOutput").ap()
        dbg["d_hid"] = nc.dram_tensor("d_hid", [128, FC * THW], BF16, kind="ExternalOutput").ap()
        dbg["d_xallg"] = nc.dram_tensor("d_xallg", [128, THW], BF16, kind="ExternalOutput").ap()
        dbg["d_xallu"] = nc.dram_tensor("d_xallu", [128, THW], BF16, kind="ExternalOutput").ap()
        dbg["d_xalld"] = nc.dram_tensor("d_xalld", [128, THW], BF16, kind="ExternalOutput").ap()
    cc_in = nc.dram_tensor("cc_in", [H * 128, 129], F32)
    cc_out = nc.dram_tensor("cc_out", [H * 128, 129], F32)
    scratch = nc.dram_tensor("ffn_scratch", [D, T], F32)
    dump3 = nc.dram_tensor("dump3", [T, D], F32)

    st = ExitStack()
    P = {}
    with st:
        def sb(name, shape, dt=F32, stk=None, side=None):
            return (stk or st).enter_context(nc.sbuf_tensor(name, shape, dt, side=side))

        pe = Ctr(st.enter_context(nc.semaphore("pe")))
        dv = Ctr(st.enter_context(nc.semaphore("dv")))
        ac = Ctr(st.enter_context(nc.semaphore("ac")))
        dg = Ctr(st.enter_context(nc.semaphore("dg")))
        cc = Ctr(st.enter_context(nc.semaphore("cc")))
        xld = [Ctr(st.enter_context(nc.semaphore(f"xld{i}"))) for i in range(2)]
        kvs = [Ctr(st.enter_context(nc.semaphore(f"kvs{i}"))) for i in range(2)]
        qcs = [Ctr(st.enter_context(nc.semaphore(f"qcs{i}"))) for i in range(2)]
        kk = Ctr(st.enter_context(nc.semaphore("kk")))
        pc = Ctr(st.enter_context(nc.semaphore("pc")))
        osts = [Ctr(st.enter_context(nc.semaphore(f"ost{i}"))) for i in range(2)]
        ds = xld[0]
        do = cc
        kst = kk
        kbf = kk
        gus = [Ctr(st.enter_context(nc.semaphore(f"gus{i}"))) for i in range(2)]
        dcs = [Ctr(st.enter_context(nc.semaphore(f"dcs{i}"))) for i in range(2)]
        wos = [Ctr(st.enter_context(nc.semaphore(f"wos{i}"))) for i in range(2)]
        fws = [Ctr(st.enter_context(nc.semaphore(f"fws{i}"))) for i in range(2)]
        frs = [Ctr(st.enter_context(nc.semaphore(f"frs{i}"))) for i in range(3)]

        blk = st.enter_context(nc.Block())

        def vpub(v, k=1):
            v.drain().then_inc(dv.sem, k)
            return dv.add(k)

        def apub(sa, k=1):
            sa.drain().then_inc(ac.sem, k)
            return ac.add(k)

        id_bf = sb("id_bf", [128, 128], BF16)
        id_f32 = sb("id_f32", [128, 128], F32)
        onesx_bf = sb("onesx_bf", [128, THW], BF16)
        pat_sb = sb("pat_sb", [E, 64], F32)
        ebias_sb = sb("ebias_sb", [128, E], F32)
        cst_sb = sb("cst_sb", [128, 8], F32)
        gpack_sb = sb("gpack_sb", [128, DC * PK], BF16)
        upack_sb = sb("upack_sb", [128, DC * PK], BF16)
        dpack_sb = sb("dpack_sb", [128, FC * PK], BF16)
        wr_sb = sb("wr_sb", [128, DC * E], BF16)
        scores = sb("scores", [128, NT * E], F32)
        w_tm = sb("w_tm", [128, NT * E], F32)
        w_fm = sb("w_fm", [E, T], F32)
        w_bcast = sb("w_bcast", [64, T], F32)
        kv_part = sb("kv_part", [128, H * 129], F32)
        kv_bf = sb("kv_bf", [128, H * 129], BF16)
        xallg = sb("xallg", [128, THW], BF16)
        xallu = sb("xallu", [128, THW], BF16)
        xalld = sb("xalld", [128, THW], BF16)
        xa_tmp = sb("xa_tmp", [128, THW], F32)
        tk = sb("tk", [128, 16], F32)

        pool = [st.enter_context(nc.psum_tensor(f"ps{i}", [128, 512], F32)) for i in range(8)]

        C_EPS8 = cst_sb[:, 0:1]
        C_EPSLN = cst_sb[:, 1:2]
        C_NEG1 = cst_sb[:, 2:3]
        C_NEGH = cst_sb[:, 3:4]

        @blk.gpsimd
        def _(gp):
            for dst, src in ((id_bf, id_h), (onesx_bf, onesx_h), (wr_sb, wr_h),
                             (gpack_sb, gpack_h), (upack_sb, upack_h), (dpack_sb, dpack_h)):
                gp.dma_start(out=dst[:, :], in_=src).then_inc(dg.sem, 16)
                dg.add(16)
        consts_g = dg.n

        @blk.sync
        def _(s):
            for dst, src in ((id_f32, id_h), (pat_sb, pat_h), (ebias_sb, ebias_h),
                             (cst_sb, cst_h)):
                s.dma_start(out=dst[:, :], in_=src).then_inc(ds.sem, 16)
                ds.add(16)
        consts_s = ds.n

        @blk.vector
        def _(v):
            v.memset(xallg[:, :], 0.0)
            v.memset(xallu[:, :], 0.0)
            v.memset(xalld[:, :], 0.0)
            vpub(v)

        stXN = ExitStack()
        xn_fm = sb("xn_fm", [128, DC * T], BF16, stXN)
        stAB = ExitStack()
        x2 = [sb(f"x2_{i}", [128, D], F32, stAB) for i in range(2)]
        xn_tm = [sb(f"xn_tm{i}", [128, D], BF16, stAB) for i in range(2)]
        stats = sb("stats", [128, 10], F32, stAB)
        k_tm = sb("k_tm", [128, NT * D], BF16, stAB)
        v_aug = sb("v_aug", [128, NT * H * 129], BF16, stAB)
        kvm = {m: [sb(f"kvm_{m}{i}", [128, 17 * 256], BF16, stAB) for i in range(2)]
               for m in ("k", "v")}
        elu_t = [sb(f"elu_t{i}", [128, 512], F32, stAB) for i in range(2)]
        qcol = [sb(f"qcol{i}", [128, 17 * 128], BF16, side="right") for i in range(2)]
        wq_v = wq_h.rearrange("(oc p) m -> oc p m", p=128)

        def emit_qcol_load(oc):
            @blk.gpsimd
            def _(gp, oc=oc):
                if oc < 2:
                    gp.wait_ge(pe.sem, P["a_end_pe"])
                    gp.wait_ge(dv.sem, xn_all)
                    gp.wait_ge(ac.sem, P["a_end_ac"])
                    gp.wait_ge(pc.sem, 128)
                else:
                    gp.wait_ge(pe.sem, P[("qmm", oc - 2)])
                gp.dma_start(out=qcol[oc % 2][:, :], in_=wq_v[oc]).then_inc(qcs[oc % 2].sem, 16)
                P[("qdma", oc)] = qcs[oc % 2].add(16)

        x_v = x_h.rearrange("(tc p) d -> tc p d", p=128)
        xn_view = xn_fm[:, :].rearrange("p (dc t) -> p dc t", t=T)

        for tc in range(NT):
            @blk.sync
            def _(s, tc=tc):
                if tc >= 2:
                    s.wait_ge(dv.sem, P[("xfree", tc - 2)])
                s.dma_start(out=x2[tc % 2][:, :], in_=x_v[tc]).then_inc(xld[tc % 2].sem, 16)
                P[("xdma", tc)] = xld[tc % 2].add(16)

            @blk.vector
            def _(v, tc=tc):
                v.wait_ge(xld[tc % 2].sem, P[("xdma", tc)])
                v.reduce_sum(stats[:, 0:1], x2[tc % 2][:, :], AX.X)
                P[("sum", tc)] = vpub(v)

            @blk.scalar
            def _(sa, tc=tc):
                sa.wait_ge(xld[tc % 2].sem, P[("xdma", tc)])
                sa.wait_ge(ds.sem, consts_s)
                sa.activation(xn_tm[tc % 2][:, :], x2[tc % 2][:, :], AF.Square,
                              accum_out=stats[:, 1:2])
                P[("sq", tc)] = apub(sa)

            @blk.vector
            def _(v, tc=tc):
                v.wait_ge(ac.sem, P[("sq", tc)])
                v.wait_ge(dv.sem, P[("sum", tc)])
                v.tensor_scalar(stats[:, 2:3], stats[:, 0:1], 1.0 / D, None, ALU.mult)
                v.tensor_scalar(stats[:, 3:4], stats[:, 1:2], 1.0 / D, None, ALU.mult)
                n = vpub(v)
                v.wait_ge(dv.sem, n)
                v.tensor_mul(stats[:, 4:5], stats[:, 2:3], stats[:, 2:3])
                n = vpub(v)
                v.wait_ge(dv.sem, n)
                v.tensor_sub(stats[:, 5:6], stats[:, 3:4], stats[:, 4:5])
                P[("var", tc)] = vpub(v)

            @blk.scalar
            def _(sa, tc=tc):
                sa.wait_ge(dv.sem, P[("var", tc)])
                sa.activation(stats[:, 6:7], stats[:, 5:6], AF.Ln, bias=C_EPSLN)
                n = apub(sa)
                sa.wait_ge(ac.sem, n)
                sa.activation(stats[:, 7:8], stats[:, 6:7], AF.Exp, scale=C_NEGH)
                P[("rstd", tc)] = apub(sa)

            @blk.vector
            def _(v, tc=tc):
                v.wait_ge(ac.sem, P[("rstd", tc)])
                v.tensor_scalar(xn_tm[tc % 2][:, :], x2[tc % 2][:, :],
                                stats[:, 2:3], stats[:, 7:8], ALU.subtract, ALU.mult)
                P[("xn", tc)] = P[("xfree", tc)] = vpub(v)

            @blk.sync
            def _(s, tc=tc):
                s.wait_ge(dv.sem, P[("xn", tc)])
                s.dma_start(out=out_h[tc * 128:(tc + 1) * 128, :],
                            in_=x2[tc % 2][:, :]).then_inc(pc.sem, 16)
                P[("pcst", tc)] = pc.add(16)

            for gq in range(4):
                bank = 6 + (gq % 2)

                @blk.tensor
                def _(t, tc=tc, gq=gq, bank=bank):
                    t.wait_ge(dv.sem, P[("xn", tc)])
                    t.wait_ge(dg.sem, consts_g)
                    key = ("trfree", bank)
                    if key in P:
                        t.wait_ge(dv.sem, P[key])
                    for q in range(4):
                        inst = t.matmul(pool[bank][:, q * 128:(q + 1) * 128],
                                        xn_tm[tc % 2][:, (4 * gq + q) * 128:(4 * gq + q + 1) * 128],
                                        id_bf[:, :], start=True, stop=True)
                    inst.then_inc(pe.sem, 1)
                    P[("trmm", tc, gq)] = pe.add(1)

                @blk.vector
                def _(v, tc=tc, gq=gq, bank=bank):
                    v.wait_ge(pe.sem, P[("trmm", tc, gq)])
                    dst = xn_view[:, 4 * gq:4 * gq + 4, tc * 128:(tc + 1) * 128]
                    src = pool[bank][:, :].rearrange("p (q n) -> p q n", n=128)
                    v.tensor_copy(dst, src)
                    P[("trfree", bank)] = vpub(v)
            P[("xnready", tc)] = P[("trfree", 6 + (3 % 2))]
        xn_all = dv.n
        P["a_end_pe"] = pe.n
        P["a_end_ac"] = ac.n
        P["a_end_pe"] = pe.n
        P["a_end_ac"] = ac.n

        wkx_v = wkx_h.rearrange("(oq p) m -> oq p m", p=128)
        wvx_v = wvx_h.rearrange("(oq p) m -> oq p m", p=128)
        NQ = 8

        for oq in range(NQ):
            if oq == 6:
                emit_qcol_load(0)
                emit_qcol_load(1)

            @blk.gpsimd
            def _(gp, oq=oq):
                if oq >= 2:
                    gp.wait_ge(pe.sem, P[("kvbuf", oq - 2)])
                for mv, srcv in (("k", wkx_v), ("v", wvx_v)):
                    gp.dma_start(out=kvm[mv][oq % 2][:, :],
                                 in_=srcv[oq]).then_inc(kvs[oq % 2].sem, 16)
                    kvs[oq % 2].add(16)
                P[("kvdma", oq)] = kvs[oq % 2].n

            for tc in range(NT):
                @blk.tensor
                def _(t, oq=oq, tc=tc):
                    t.wait_ge(kvs[oq % 2].sem, P[("kvdma", oq)])
                    if oq == 0:
                        t.wait_ge(dv.sem, P[("xnready", tc)])
                    for key in (("kbank", tc % 2), ("vbank", tc % 2)):
                        if key in P:
                            t.wait_ge(dv.sem, P[key])
                    if oq == 0 and ("rbank", tc % 2) in P:
                        t.wait_ge(ac.sem, P[("rbank", tc % 2)])
                    for c in range(17):
                        stat = (xn_fm[:, c * T + tc * 128:c * T + (tc + 1) * 128]
                                if c < 16 else onesx_bf[:, 0:128])
                        t.matmul(pool[tc % 2][:, 0:256], stat,
                                 kvm["k"][oq % 2][:, c * 256:(c + 1) * 256],
                                 start=(c == 0), stop=(c == 16), skip_group_check=True)
                        iV = t.matmul(pool[2 + tc % 2][:, 0:256], stat,
                                      kvm["v"][oq % 2][:, c * 256:(c + 1) * 256],
                                      start=(c == 0), stop=(c == 16), skip_group_check=True)
                        if oq == 0 and c < 16:
                            iR = t.matmul(pool[4 + tc % 2][:, 0:E], stat,
                                          wr_sb[:, c * E:(c + 1) * E],
                                          start=(c == 0), stop=(c == 15),
                                          skip_group_check=True)
                            if c == 15:
                                iR.then_inc(pe.sem, 1)
                                P[("rmm", tc)] = pe.add(1)
                    iV.then_inc(pe.sem, 1)
                    P[("kvmm", oq, tc)] = pe.add(1)
                    if tc == NT - 1:
                        P[("kvbuf", oq)] = pe.n

                @blk.scalar
                def _(sa, oq=oq, tc=tc):
                    sa.wait_ge(pe.sem, P[("kvmm", oq, tc)])
                    key = ("kbank", tc % 2)
                    if key in P:
                        sa.wait_ge(dv.sem, P[key])
                    sa.activation(elu_t[tc % 2][:, 0:256], pool[tc % 2][:, 0:256], AF.Exp)
                    P[("kexp", oq, tc)] = apub(sa)

                @blk.vector
                def _(v, oq=oq, tc=tc):
                    v.wait_ge(ac.sem, P[("kexp", oq, tc)])
                    v.tensor_scalar(elu_t[tc % 2][:, 256:512], elu_t[tc % 2][:, 0:256],
                                    1.0, None, ALU.min)
                    n = vpub(v)
                    v.wait_ge(dv.sem, n)
                    v.scalar_tensor_tensor(
                        k_tm[:, tc * D + oq * 256: tc * D + (oq + 1) * 256],
                        pool[tc % 2][:, 0:256], 0.0, elu_t[tc % 2][:, 256:512],
                        ALU.max, ALU.add)
                    P[("kbank", tc % 2)] = vpub(v)
                    va = v_aug[:, :].rearrange("p (tc h e) -> p tc h e", h=H, e=129)
                    v.tensor_copy(va[:, tc, 2 * oq:2 * oq + 2, 0:128],
                                  pool[2 + tc % 2][:, 0:256].rearrange(
                                      "p (h e) -> p h e", e=128))
                    P[("vbank", tc % 2)] = vpub(v)

                if oq == 0:
                    @blk.scalar
                    def _(sa, tc=tc):
                        sa.wait_ge(pe.sem, P[("rmm", tc)])
                        sa.activation(scores[:, tc * E:(tc + 1) * E],
                                      pool[4 + tc % 2][:, 0:E], AF.Sigmoid)
                        P[("sig", tc)] = P[("rbank", tc % 2)] = apub(sa)

        @blk.vector
        def _(v):
            va = v_aug[:, :].rearrange("p (tc h e) -> p tc h e", h=H, e=129)
            v.memset(va[:, :, :, 128:129], 1.0)
            P["vones"] = vpub(v)

        for h in range(H):
            @blk.tensor
            def _(t, h=h):
                t.wait_ge(dv.sem, P["vones"])
                if h >= 2:
                    t.wait_ge(dv.sem, P[("kvevac", h - 2)])
                else:
                    t.wait_ge(ac.sem, P[("sig", NT - 1)])
                t.wait_ge(dv.sem, max(P[("kbank", 0)], P[("kbank", 1)],
                                      P[("vbank", 0)], P[("vbank", 1)]))
                for tc in range(NT):
                    inst = t.matmul(
                        pool[4 + h % 2][:, 0:129],
                        k_tm[:, tc * D + h * 128: tc * D + (h + 1) * 128],
                        v_aug[:, tc * H * 129 + h * 129: tc * H * 129 + (h + 1) * 129],
                        start=(tc == 0), stop=(tc == NT - 1))
                inst.then_inc(pe.sem, 1)
                P[("kvh", h)] = pe.add(1)

            @blk.vector
            def _(v, h=h):
                v.wait_ge(pe.sem, P[("kvh", h)])
                v.tensor_copy(kv_part[:, h * 129:(h + 1) * 129], pool[4 + h % 2][:, 0:129])
                P[("kvevac", h)] = vpub(v)

        @blk.sync
        def _(s):
            s.wait_ge(dv.sem, P[("kvevac", H - 1)])
            dst = cc_in.ap().rearrange("(h d) e -> d h e", d=128)
            src = kv_part[:, :].rearrange("p (h e) -> p h e", e=129)
            s.dma_start(out=dst[:, 0:8, :], in_=src[:, 0:8, :]).then_inc(kst.sem, 16)
            s.dma_start(out=dst[:, 8:16, :], in_=src[:, 8:16, :]).then_inc(kst.sem, 16)
            P["kvstore"] = kst.add(32)

        if DEBUG_DUMP:
            @blk.sync
            def _(s):
                s.wait_ge(dv.sem, dv.n)
                s.wait_ge(ac.sem, ac.n)
                s.dma_start(out=dbg["d_xn"], in_=xn_fm[:, :]).then_inc(do.sem, 16)
                s.dma_start(out=dbg["d_ktm"], in_=k_tm[:, :]).then_inc(do.sem, 16)
                s.dma_start(out=dbg["d_vaug"], in_=v_aug[:, :]).then_inc(do.sem, 16)
                s.dma_start(out=dbg["d_kvp"], in_=kv_part[:, :]).then_inc(do.sem, 16)
                s.dma_start(out=dbg["d_scores"], in_=scores[:, :]).then_inc(do.sem, 16)
                do.add(5 * 16)
                s.wait_ge(do.sem, do.n)
            P["dbg1"] = do.n
        b_end_pe = pe.n
        b_end_dv = dv.n
        stAB.close()

        for tc in range(NT):
            @blk.vector
            def _(v, tc=tc):
                sc = scores[:, tc * E:(tc + 1) * E]
                v.wait_ge(ac.sem, P[("sig", tc)])
                v.tensor_add(tk[:, 0:8], sc, ebias_sb[:, :])
                n = vpub(v)
                v.wait_ge(dv.sem, n)
                v.reduce_max(tk[:, 8:9], tk[:, 0:8], AX.X)
                n = vpub(v)
                v.wait_ge(dv.sem, n)
                v.tensor_scalar(tk[:, 0:8], tk[:, 0:8], tk[:, 8:9], None, ALU.is_equal)
                P[("eq", tc)] = vpub(v)

            @blk.vector
            def _(v, tc=tc):
                sc = scores[:, tc * E:(tc + 1) * E]
                v.wait_ge(dv.sem, P[("eq", tc)])
                v.tensor_add(xa_tmp[:, 0:8], sc, ebias_sb[:, :])
                n = vpub(v)
                v.wait_ge(dv.sem, n)
                v.scalar_tensor_tensor(xa_tmp[:, 8:16], tk[:, 0:8], -1e30,
                                       xa_tmp[:, 0:8], ALU.mult, ALU.add)
                n = vpub(v)
                v.wait_ge(dv.sem, n)
                v.reduce_max(tk[:, 8:9], xa_tmp[:, 8:16], AX.X)
                n = vpub(v)
                v.wait_ge(dv.sem, n)
                v.tensor_scalar(xa_tmp[:, 0:8], xa_tmp[:, 0:8], tk[:, 8:9], None,
                                ALU.is_ge)
                n = vpub(v)
                v.wait_ge(dv.sem, n)
                v.tensor_mul(w_tm[:, tc * E:(tc + 1) * E], sc, xa_tmp[:, 0:8])
                n = vpub(v)
                v.wait_ge(dv.sem, n)
                v.reduce_sum(tk[:, 8:9], w_tm[:, tc * E:(tc + 1) * E], AX.X)
                P[("den", tc)] = vpub(v)

            @blk.scalar
            def _(sa, tc=tc):
                sa.wait_ge(dv.sem, P[("den", tc)])
                sa.activation(tk[:, 10:11], tk[:, 8:9], AF.Ln, bias=C_EPS8)
                n = apub(sa)
                sa.wait_ge(ac.sem, n)
                sa.activation(tk[:, 11:12], tk[:, 10:11], AF.Exp, scale=C_NEG1)
                P[("denr", tc)] = apub(sa)

            @blk.vector
            def _(v, tc=tc):
                v.wait_ge(ac.sem, P[("denr", tc)])
                v.tensor_scalar(w_tm[:, tc * E:(tc + 1) * E], w_tm[:, tc * E:(tc + 1) * E],
                                tk[:, 11:12], None, ALU.mult)
                P[("wtm", tc)] = vpub(v)

        stQ = ExitStack()
        q_fm = sb("q_fm", [128, H * T], BF16, stQ, side="right")
        stC = ExitStack()
        qt = [sb("qt0", [128, 512], F32, stC)] * 2

        for oc in range(H):
            if oc >= 2:
                emit_qcol_load(oc)

            @blk.tensor
            def _(t, oc=oc):
                t.wait_ge(qcs[oc % 2].sem, P[("qdma", oc)])
                for b in range(2):
                    key = ("qbank", (oc % 2) * 2 + b)
                    if key in P:
                        t.wait_ge(dv.sem, P[key])
                    elif oc < 2:
                        t.wait_ge(dv.sem, b_end_dv)
                for c in range(17):
                    for th in range(TH):
                        bank = (oc % 2) * 2 + th
                        mov = (xn_fm[:, c * T + th * THW: c * T + (th + 1) * THW]
                               if c < 16 else onesx_bf[:, :])
                        inst = t.matmul(pool[bank][:, :],
                                        qcol[oc % 2][:, c * 128:(c + 1) * 128],
                                        mov, start=(c == 0), stop=(c == 16),
                                        skip_group_check=True)
                inst.then_inc(pe.sem, 1)
                P[("qmm", oc)] = pe.add(1)

            for th in range(TH):
                bank = (oc % 2) * 2 + th

                @blk.scalar
                def _(sa, oc=oc, th=th, bank=bank):
                    sa.wait_ge(pe.sem, P[("qmm", oc)])
                    key = "qtmp"
                    if key in P:
                        sa.wait_ge(dv.sem, P[key])
                    sa.activation(qt[th][:, :], pool[bank][:, :], AF.Exp)
                    P[("qexp", oc, th)] = apub(sa)

                @blk.vector
                def _(v, oc=oc, th=th, bank=bank):
                    v.wait_ge(ac.sem, P[("qexp", oc, th)])
                    v.tensor_scalar(qt[th][:, :], qt[th][:, :], 1.0, None, ALU.min)
                    n = vpub(v)
                    v.wait_ge(dv.sem, n)
                    v.scalar_tensor_tensor(
                        q_fm[:, oc * T + th * THW: oc * T + (th + 1) * THW],
                        pool[bank][:, :], 0.0, qt[th][:, :], ALU.max, ALU.add)
                    P[("qbank", bank)] = P["qtmp"] = vpub(v)
        for tc in range(NT):
            @blk.tensor
            def _(t, tc=tc):
                t.wait_ge(dv.sem, P[("wtm", tc)])
                t.wait_ge(ds.sem, consts_s)
                if tc >= 2:
                    t.wait_ge(dv.sem, P[("wfm", tc - 2)])
                else:
                    t.wait_ge(dv.sem, max(P[("qbank", 2)], P[("qbank", 3)]))
                inst = t.matmul(pool[2 + tc % 2][0:E, 0:128],
                                w_tm[:, tc * E:(tc + 1) * E], id_f32[:, :],
                                start=True, stop=True)
                inst.then_inc(pe.sem, 1)
                P[("wtr", tc)] = pe.add(1)

            @blk.vector
            def _(v, tc=tc):
                v.wait_ge(pe.sem, P[("wtr", tc)])
                v.tensor_copy(w_fm[:, tc * 128:(tc + 1) * 128], pool[2 + tc % 2][0:E, 0:128])
                P[("wfm", tc)] = vpub(v)

        for half in range(2):
            @blk.tensor
            def _(t, half=half):
                t.wait_ge(dv.sem, P[("wfm", NT - 1)])
                inst = t.matmul(pool[2 + half][0:64, 0:512], pat_sb[:, :],
                                w_fm[:, half * 512:(half + 1) * 512],
                                start=True, stop=True)
                inst.then_inc(pe.sem, 1)
                P[("wbc_mm", half)] = pe.add(1)

            @blk.vector
            def _(v, half=half):
                v.wait_ge(pe.sem, P[("wbc_mm", half)])
                v.tensor_copy(w_bcast[:, half * 512:(half + 1) * 512],
                              pool[2 + half][0:64, 0:512])
                P[("wbc", half)] = vpub(v)

        P["qdone_pe"] = pe.n
        c_end_dv = dv.n

        @blk.gpsimd
        def _(gp):
            gp.wait_ge(kst.sem, P["kvstore"])
            gp.collective_compute(
                "AllReduce", ALU.add,
                replica_groups=[[0, 1, 2, 3], [4, 5, 6, 7]],
                ins=[cc_in.ap().opt()], outs=[cc_out.ap().opt()],
            ).then_inc(cc.sem)
            cc.add(1)

        def emit_kv_readback():
            @blk.gpsimd
            def _(gp):
                gp.wait_ge(cc.sem, 1)
                dst = kv_bf[:, :].rearrange("p (h e) -> p h e", e=129)
                srcv = cc_out.ap().rearrange("(h d) e -> d h e", d=128)
                gp.dma_start(out=dst[:, 0:8, :], in_=srcv[:, 0:8, :]).then_inc(kbf.sem, 16)
                gp.dma_start(out=dst[:, 8:16, :], in_=srcv[:, 8:16, :]).then_inc(kbf.sem, 16)
                P["kvbf"] = kbf.add(32)

        if DEBUG_DUMP:
            @blk.sync
            def _(s):
                s.wait_ge(dv.sem, dv.n)
                s.wait_ge(kbf.sem, P["kvbf"])
                s.dma_start(out=dbg["d_qfm"], in_=q_fm[:, :]).then_inc(do.sem, 16)
                s.dma_start(out=dbg["d_wtm"], in_=w_tm[:, :]).then_inc(do.sem, 16)
                s.dma_start(out=dbg["d_wb"], in_=w_bcast[:, :]).then_inc(do.sem, 16)
                s.dma_start(out=dbg["d_kvbf"], in_=kv_bf[:, :]).then_inc(do.sem, 16)
                do.add(4 * 16)
                s.wait_ge(do.sem, do.n)

        stE = ExitStack()
        gcol = [sb(f"gcol{i}", [128, 17 * 128], BF16, stE) for i in range(2)]
        ucol = [sb(f"ucol{i}", [128, 17 * 128], BF16, stE) for i in range(2)]
        dcol = [sb(f"dcol{i}", [128, 44 * 128], BF16, stE) for i in range(2)]
        hid = sb("hid", [128, FC * THW], BF16, stE)
        silu_t = [sb(f"silu{i}", [128, THW], F32, stE) for i in range(2)]
        fstage = [sb(f"fstage{i}", [128, THW], F32, stE) for i in range(2)]
        t1row = sb("t1row", [1, THW], F32, stE)
        rrow = [sb(f"rrow{i}", [1, THW], BF16, stE) for i in range(2)]
        gcols_v = gcols_h.rearrange("(fc p) m -> fc p m", p=128)
        ucols_v = ucols_h.rearrange("(fc p) m -> fc p m", p=128)
        dcols_v = dcols_h.rearrange("(dc p) m -> dc p m", p=128)

        attn_fm = xn_fm
        qt_bf = qt[0][:, :].bitcast(BF16)
        qkv_sc = [qt_bf[:, 0:THW], qt_bf[:, THW:2 * THW]]
        si_bf = silu_t[0][:, :].bitcast(BF16)
        bc_bf = [si_bf[:, 0:THW], si_bf[:, THW:2 * THW]]

        def emit_attn_mm(it):
            thh, hh = divmod(it, H)

            @blk.tensor
            def _(t, it=it, thh=thh, hh=hh):
                t.wait_ge(kbf.sem, P["kvbf"])
                if it < 2:
                    t.wait_ge(dv.sem, P[("xad_ev", 1)])
                if ("qkvfreeE", it % 2) in P:
                    t.wait_ge(ac.sem, P[("qkvfreeE", it % 2)])
                if "denfreeE" in P:
                    t.wait_ge(ac.sem, P["denfreeE"])
                mov = q_fm[:, hh * T + thh * THW: hh * T + (thh + 1) * THW]
                t.matmul(pool[2 + it % 2][:, :], kv_bf[:, hh * 129:hh * 129 + 128],
                         mov, start=True, stop=True)
                inst = t.matmul(pool[5][0:1, :],
                                kv_bf[:, hh * 129 + 128:hh * 129 + 129],
                                mov, start=True, stop=True)
                inst.then_inc(pe.sem, 1)
                P[("a2mm", it)] = pe.add(1)

            @blk.scalar
            def _(sa, it=it):
                sa.wait_ge(pe.sem, P[("a2mm", it)])
                if it >= 2:
                    sa.wait_ge(pe.sem, P[("a2bc", it - 2)])
                sa.activation(t1row[:, :], pool[5][0:1, :], AF.Ln, bias=cst_sb[0:1, 0:1])
                P["denfreeE"] = n = apub(sa)
                sa.wait_ge(ac.sem, n)
                sa.activation(rrow[it % 2][:, :], t1row[:, :], AF.Exp, scale=cst_sb[0:1, 2:3])
                P[("a2rr", it)] = apub(sa)

        def emit_attn_bc(it):
            thh, hh = divmod(it, H)

            @blk.tensor
            def _(t, it=it):
                t.wait_ge(ac.sem, P[("a2rr", it)])
                if ("bcfreeE", it % 2) in P:
                    t.wait_ge(ac.sem, P[("bcfreeE", it % 2)])
                inst = t.matmul(pool[6 + it % 2][:, :], onesx_bf[0:1, 0:128],
                                rrow[it % 2][:, :], start=True, stop=True)
                inst.then_inc(pe.sem, 1)
                P[("a2bc", it)] = pe.add(1)

            @blk.scalar
            def _(sa, it=it):
                sa.wait_ge(pe.sem, P[("a2bc", it)])
                if it >= 2:
                    sa.wait_ge(kk.sem, P[("gmul", it - 2)])
                sa.activation(qkv_sc[it % 2], pool[2 + it % 2][:, :], AF.Identity)
                sa.activation(bc_bf[it % 2], pool[6 + it % 2][:, :], AF.Identity)
                P[("qkvfreeE", it % 2)] = P[("bcfreeE", it % 2)] = \
                    P[("aevac", it)] = apub(sa)

            @blk.gpsimd
            def _(gp, it=it, thh=thh, hh=hh):
                gp.wait_ge(ac.sem, P[("aevac", it)])
                gp.tensor_mul(attn_fm[:, hh * T + thh * THW: hh * T + (thh + 1) * THW],
                              qkv_sc[it % 2], bc_bf[it % 2])
                gp.drain().then_inc(kk.sem, 1)
                P[("gmul", it)] = kk.add(1)

        if not SKIP_FFN:
            for th in range(TH):
                for nm, pk_sb, bank in (("g", gpack_sb, 4), ("u", upack_sb, 5)):
                    @blk.tensor
                    def _(t, th=th, nm=nm, pk_sb=pk_sb, bank=bank):
                        if th == 0 and nm == "g":
                            t.wait_ge(pe.sem, P["qdone_pe"])
                            t.wait_ge(dv.sem, c_end_dv)
                        if th == 1:
                            t.wait_ge(dv.sem, P[("xa_ev", nm, 0)])
                        for c in range(DC):
                            inst = t.matmul(pool[bank][0:PK, 0:THW],
                                            pk_sb[:, c * PK:(c + 1) * PK],
                                            xn_fm[:, c * T + th * THW: c * T + (th + 1) * THW],
                                            start=(c == 0), stop=(c == DC - 1))
                        inst.then_inc(pe.sem, 1)
                        P[("xamm", nm, th)] = pe.add(1)

                    @blk.vector
                    def _(v, th=th, nm=nm, bank=bank):
                        xall = xallg if nm == "g" else xallu
                        v.wait_ge(pe.sem, P[("xamm", nm, th)])
                        v.wait_ge(dv.sem, max(P[("wbc", 0)], P[("wbc", 1)]))
                        if th == 1:
                            v.wait_ge(pe.sem, P[("gumm", 42, th - 1)])
                        v.tensor_scalar(xa_tmp[0:64, 0:THW], pool[bank][0:64, 0:THW],
                                        0.125, None, ALU.mult)
                        n = vpub(v)
                        v.wait_ge(dv.sem, n)
                        v.tensor_mul(xall[0:64, :], xa_tmp[0:64, 0:THW],
                                     w_bcast[0:64, th * THW:(th + 1) * THW])
                        n = vpub(v)
                        v.tensor_scalar(xall[64:PK, :], pool[bank][64:PK, 0:THW],
                                        0.125, None, ALU.mult)
                        P[("xa_ev", nm, th)] = vpub(v)

                for fc in range(FC):
                    @blk.gpsimd
                    def _(gp, th=th, fc=fc):
                        idx = th * FC + fc
                        if idx >= 2:
                            pth, pfc = divmod(idx - 2, FC)
                            gp.wait_ge(pe.sem, P[("gumm", pfc, pth)])
                        if idx < 2:
                            gp.wait_ge(pe.sem, b_end_pe)
                        gp.dma_start(out=gcol[fc % 2][:, :], in_=gcols_v[fc]).then_inc(gus[fc % 2].sem, 16)
                        gp.dma_start(out=ucol[fc % 2][:, :], in_=ucols_v[fc]).then_inc(gus[fc % 2].sem, 16)
                        P[("gudma", fc, th)] = gus[fc % 2].add(32)

                    @blk.tensor
                    def _(t, th=th, fc=fc):
                        t.wait_ge(gus[fc % 2].sem, P[("gudma", fc, th)])
                        t.wait_ge(dv.sem, P[("xa_ev", "g", th)])
                        t.wait_ge(dv.sem, P[("xa_ev", "u", th)])
                        kk = ("gubank", fc % 2)
                        if kk in P:
                            t.wait_ge(dv.sem, P[kk])
                        for c in range(17):
                            movg = (xn_fm[:, c * T + th * THW: c * T + (th + 1) * THW]
                                    if c < 16 else xallg[:, :])
                            movu = (xn_fm[:, c * T + th * THW: c * T + (th + 1) * THW]
                                    if c < 16 else xallu[:, :])
                            t.matmul(pool[fc % 2][:, :], gcol[fc % 2][:, c * 128:(c + 1) * 128],
                                     movg, start=(c == 0), stop=(c == 16), skip_group_check=True)
                            iU = t.matmul(pool[2 + fc % 2][:, :],
                                          ucol[fc % 2][:, c * 128:(c + 1) * 128],
                                          movu, start=(c == 0), stop=(c == 16),
                                          skip_group_check=True)
                        if fc >= 2:
                            j = fc - 2
                            t.matmul(pool[4][0:PK, 0:THW],
                                     dpack_sb[:, j * PK:(j + 1) * PK],
                                     hid[:, j * THW:(j + 1) * THW],
                                     start=(j == 0), stop=False, skip_group_check=True)
                        iU.then_inc(pe.sem, 1)
                        P[("gumm", fc, th)] = pe.add(1)

                    @blk.scalar
                    def _(sa, th=th, fc=fc):
                        sa.wait_ge(pe.sem, P[("gumm", fc, th)])
                        key = ("gubank", fc % 2)
                        if key in P:
                            sa.wait_ge(dv.sem, P[key])
                        sa.activation(silu_t[fc % 2][:, :], pool[fc % 2][:, :], AF.Silu)
                        P[("silu", fc, th)] = apub(sa)

                    @blk.vector
                    def _(v, th=th, fc=fc):
                        v.wait_ge(ac.sem, P[("silu", fc, th)])
                        v.tensor_mul(hid[:, fc * THW:(fc + 1) * THW],
                                     silu_t[fc % 2][:, :], pool[2 + fc % 2][:, :])
                        P[("gubank", fc % 2)] = P[("hidp", fc, th)] = vpub(v)

                @blk.tensor
                def _(t, th=th):
                    for j in (FC - 2, FC - 1):
                        t.wait_ge(dv.sem, P[("hidp", j, th)])
                        inst = t.matmul(pool[4][0:PK, 0:THW],
                                        dpack_sb[:, j * PK:(j + 1) * PK],
                                        hid[:, j * THW:(j + 1) * THW],
                                        start=False, stop=(j == FC - 1),
                                        skip_group_check=True)
                    inst.then_inc(pe.sem, 1)
                    P[("xadmm", th)] = pe.add(1)

                @blk.vector
                def _(v, th=th):
                    v.wait_ge(pe.sem, P[("xadmm", th)])
                    v.tensor_scalar(xa_tmp[0:64, 0:THW], pool[4][0:64, 0:THW],
                                    0.125, None, ALU.mult)
                    n = vpub(v)
                    v.wait_ge(dv.sem, n)
                    v.tensor_mul(xalld[0:64, :], xa_tmp[0:64, 0:THW],
                                 w_bcast[0:64, th * THW:(th + 1) * THW])
                    n = vpub(v)
                    v.tensor_scalar(xalld[64:PK, :], pool[4][64:PK, 0:THW],
                                    0.125, None, ALU.mult)
                    P[("xad_ev", th)] = vpub(v)

                if DEBUG_DUMP and th == 0:
                    @blk.sync
                    def _(s):
                        s.wait_ge(dv.sem, P[("xad_ev", 0)])
                        s.dma_start(out=dbg["d_hid"], in_=hid[:, :]).then_inc(do.sem, 16)
                        s.dma_start(out=dbg["d_xallg"], in_=xallg[:, :]).then_inc(do.sem, 16)
                        s.dma_start(out=dbg["d_xallu"], in_=xallu[:, :]).then_inc(do.sem, 16)
                        s.dma_start(out=dbg["d_xalld"], in_=xalld[:, :]).then_inc(do.sem, 16)
                        do.add(4 * 16)
                        s.wait_ge(do.sem, do.n)

                for dc in range(DC):
                    @blk.gpsimd
                    def _(gp, th=th, dc=dc):
                        idx = th * DC + dc
                        if idx >= 2:
                            pth, pdc = divmod(idx - 2, DC)
                            gp.wait_ge(pe.sem, P[("dmm", pdc, pth)])
                        if idx == 0:
                            gp.wait_ge(pe.sem, P[("gumm", FC - 1, 0)])
                        gp.dma_start(out=dcol[dc % 2][:, :], in_=dcols_v[dc]).then_inc(dcs[dc % 2].sem, 16)
                        P[("ddma", dc, th)] = dcs[dc % 2].add(16)
                    if th == 1:
                        if dc >= 1:
                            emit_attn_bc(2 * dc - 2)
                            emit_attn_bc(2 * dc - 1)
                        emit_attn_mm(2 * dc)
                        emit_attn_mm(2 * dc + 1)

                    @blk.tensor
                    def _(t, th=th, dc=dc):
                        t.wait_ge(dcs[dc % 2].sem, P[("ddma", dc, th)])
                        t.wait_ge(dv.sem, P[("xad_ev", th)])
                        key = ("dbank", dc % 2)
                        if key in P:
                            t.wait_ge(dv.sem, P[key])
                        for c in range(44):
                            mov = (hid[:, c * THW:(c + 1) * THW] if c < FC else xalld[:, :])
                            inst = t.matmul(pool[dc % 2][:, :],
                                            dcol[dc % 2][:, c * 128:(c + 1) * 128],
                                            mov, start=(c == 0), stop=(c == 43))
                        inst.then_inc(pe.sem, 1)
                        P[("dmm", dc, th)] = pe.add(1)

                    @blk.vector
                    def _(v, th=th, dc=dc):
                        v.wait_ge(pe.sem, P[("dmm", dc, th)])
                        if ("fst", dc % 2) in P:
                            v.wait_ge(fws[dc % 2].sem, P[("fst", dc % 2)])
                        v.tensor_copy(fstage[dc % 2][:, :], pool[dc % 2][:, :])
                        P[("dbank", dc % 2)] = P[("fev", dc, th)] = vpub(v)

                    @blk.sync
                    def _(s, th=th, dc=dc):
                        s.wait_ge(dv.sem, P[("fev", dc, th)])
                        s.dma_start(out=scratch.ap()[dc * 128:(dc + 1) * 128,
                                                     th * THW:(th + 1) * THW],
                                    in_=fstage[dc % 2][:, :]).then_inc(fws[dc % 2].sem, 16)
                        P[("fst", dc % 2)] = P[("fsc", dc, th)] = fws[dc % 2].add(16)
                if th == 0:
                    emit_kv_readback()
                else:
                    emit_attn_bc(2 * DC - 2)
                    emit_attn_bc(2 * DC - 1)
        e_end_pe = pe.n
        e_end_dv = dv.n
        stE.close()
        stC.close()

        stD = ExitStack()
        wocol = [sb(f"wocol{i}", [128, 17 * 128], BF16, stD) for i in range(2)]
        wocol32 = [sb(f"wocol32_{i}", [128, 17 * 128], F32, stD) for i in range(2)]
        s1 = [sb(f"s1_{i}", [128, THW], F32, stD) for i in range(2)]
        fbuf = [sb(f"fbuf{i}", [128, THW], F32, stD) for i in range(3)]
        ostage = sb("ostage", [128, 4 * D], F32, stD)
        wo_v = wo_h.rearrange("(dc p) m -> dc p m", p=128)
        out_q = out_h.rearrange("(q p) d -> p q d", p=128)

        attn_ready = P[("gmul", 2 * DC - 1)]

        if DEBUG_DUMP:
            @blk.sync
            def _(s):
                s.wait_ge(kk.sem, attn_ready)
                s.dma_start(out=dbg["d_attn"], in_=attn_fm[:, :]).then_inc(do.sem, 16)
                if not SKIP_FFN:
                    s.wait_ge(ds.sem, P[("fsc", DC - 1, TH - 1)])
                    s.dma_start(out=dbg["d_ffn"], in_=scratch.ap()).then_inc(do.sem, 16)
                    do.add(16)
                do.add(16)
                s.wait_ge(do.sem, do.n)

        def emit_otr(jdx):
            jth, jdc = divmod(jdx, DC)

            @blk.tensor
            def _(t, jdx=jdx, jth=jth, jdc=jdc):
                t.wait_ge(dv.sem, P[("s1p", jdc, jth)])
                key = ("otrfree", jdx % 2)
                if key in P:
                    t.wait_ge(dv.sem, P[key])
                for q in range(4):
                    inst = t.transpose(pool[6 + jdx % 2][:, q * 128:(q + 1) * 128],
                                       s1[jdx % 2][:, q * 128:(q + 1) * 128],
                                       id_f32[:, :])
                inst.then_inc(pe.sem, 1)
                P[("otr", jdc, jth)] = P[("s1free", jdx % 2)] = pe.add(1)

            @blk.vector
            def _(v, jdx=jdx, jth=jth, jdc=jdc):
                v.wait_ge(pe.sem, P[("otr", jdc, jth)])
                if jth == 1 and jdc == 0:
                    v.wait_ge(osts[0].sem, 32)
                dst = ostage[:, :].rearrange("p (q d) -> p q d", d=D)[
                    :, :, jdc * 128:(jdc + 1) * 128]
                v.tensor_copy(dst, pool[6 + jdx % 2][:, :].rearrange(
                    "p (q n) -> p q n", n=128))
                P[("otrfree", jdx % 2)] = P[("oev", jdc, jth)] = vpub(v)

            if jdc % 8 == 7:
                kq = jdc // 8
                @blk.gpsimd
                def _(gp, jth=jth, kq=kq):
                    gp.wait_ge(dv.sem, P[("oev", 8 * kq + 7, jth)])
                    gp.wait_ge(pc.sem, 128)
                    dstq = out_h.rearrange("(q p) (kq n) -> p q kq n", p=128, n=1024)[
                        :, jth * 4:(jth + 1) * 4, kq, :]
                    srcq = ostage[:, :].rearrange("p (q kq n) -> p q kq n",
                                                  kq=2, n=1024)[:, :, kq, :]
                    gp.dma_start(out=dstq, in_=srcq,
                                 accum_op=ALU.add).then_inc(osts[jth].sem, 16)
                    P[("oacc", jth)] = osts[jth].add(16)

        for th in range(TH):
            for dc in range(DC):
                idx = th * DC + dc

                @blk.sync
                def _(s, th=th, dc=dc, idx=idx):
                    if idx == 0:
                        s.wait_ge(kk.sem, attn_ready)
                        s.wait_ge(ac.sem, P[("aevac", 2 * DC - 1)])
                    if idx < 2:
                        s.wait_ge(pe.sem, e_end_pe)
                    if idx >= 2:
                        s.wait_ge(ac.sem, P[("wocast", idx - 2)])
                    s.dma_start(out=wocol32[idx % 2][:, :], in_=wo_v[dc]
                                ).then_inc(wos[idx % 2].sem, 16)
                    P[("wodma", dc, th)] = wos[idx % 2].add(16)

                @blk.scalar
                def _(sa, th=th, dc=dc, idx=idx):
                    sa.wait_ge(wos[idx % 2].sem, P[("wodma", dc, th)])
                    if idx >= 2:
                        pth, pdc = divmod(idx - 2, DC)
                        sa.wait_ge(pe.sem, P[("womm", pdc, pth)])
                    sa.activation(wocol[idx % 2][:, :], wocol32[idx % 2][:, :], AF.Identity)
                    P[("wocast", idx)] = apub(sa)

                @blk.sync
                def _(s, th=th, dc=dc, idx=idx):
                    fb = idx % 3
                    if idx < 3:
                        s.wait_ge(pe.sem, e_end_pe)
                    if not SKIP_FFN:
                        s.wait_ge(fws[dc % 2].sem, P[("fsc", dc, th)])
                    if idx >= 3:
                        pth, pdc = divmod(idx - 3, DC)
                        s.wait_ge(dv.sem, P[("s1p", pdc, pth)])
                    s.dma_start(out=fbuf[fb][:, :],
                                in_=scratch.ap()[dc * 128:(dc + 1) * 128,
                                                 th * THW:(th + 1) * THW]
                                ).then_inc(frs[fb].sem, 16)
                    P[("fread", dc, th)] = frs[fb].add(16)

                @blk.tensor
                def _(t, th=th, dc=dc, idx=idx):
                    t.wait_ge(ac.sem, P[("wocast", idx)])
                    if idx == 0:
                        t.wait_ge(kk.sem, attn_ready)
                    key = ("wofree", idx % 2)
                    if key in P:
                        t.wait_ge(dv.sem, P[key])
                    for c in range(17):
                        mov = (attn_fm[:, c * T + th * THW: c * T + (th + 1) * THW]
                               if c < 16 else onesx_bf[:, :])
                        inst = t.matmul(pool[idx % 2][:, :],
                                        wocol[idx % 2][:, c * 128:(c + 1) * 128],
                                        mov, start=(c == 0), stop=(c == 16))
                    inst.then_inc(pe.sem, 1)
                    P[("womm", dc, th)] = pe.add(1)

                @blk.vector
                def _(v, th=th, dc=dc, idx=idx):
                    v.wait_ge(pe.sem, P[("womm", dc, th)])
                    v.wait_ge(frs[idx % 3].sem, P[("fread", dc, th)])
                    key = ("s1free", idx % 2)
                    if key in P:
                        v.wait_ge(pe.sem, P[key])
                    v.tensor_add(s1[idx % 2][:, :], pool[idx % 2][:, :], fbuf[idx % 3][:, :])
                    P[("wofree", idx % 2)] = P[("s1p", dc, th)] = vpub(v)

                if idx >= 1:
                    emit_otr(idx - 1)
                if idx == TH * DC - 1:
                    emit_otr(idx)

        @blk.sync
        def _(s):
            for og in range(2):
                s.wait_ge(osts[og].sem, osts[og].n)

        stD.close()
        stC2close = None
        stXN.close()
        stQ.close()
    return nc


_CACHE = {}


def kernel(**inputs):
    global LAST_RES
    xs, shared = _prep(inputs)
    key = ("nc", SKIP_FFN, SKIP_ATTN, DEBUG_DUMP)
    if key not in _CACHE:
        _CACHE[key] = build_nc()
    nc = _CACHE[key]
    in_maps = []
    for c in range(NCORES):
        m = {"x": xs[c]}
        m.update(shared)
        in_maps.append(m)
    res = run_bass_kernel_spmd(nc, in_maps, core_ids=list(range(NCORES)))
    LAST_RES = res
    out = np.empty((2, 4 * T, D), np.float32)
    for c in range(NCORES):
        out[c // 4, (c % 4) * T:(c % 4 + 1) * T, :] = res.results[c]["out"]
    return out


# version 41
# speedup vs baseline: 1.0258x; 1.0258x over previous
import numpy as np
from contextlib import ExitStack

import concourse.bass as bass
import concourse.mybir as mybir
from concourse.bass_utils import run_bass_kernel_spmd

F32 = mybir.dt.float32
BF16 = mybir.dt.bfloat16
AF = mybir.ActivationFunctionType
ALU = mybir.AluOpType
AX = mybir.AxisListType

D, DC = 2048, 16
DFF, FC = 5504, 43
T, NT = 1024, 8
TH, THW = 2, 512
H = 16
E, R, PK = 8, 8, 72
NCORES = 8

SKIP_FFN = False
SKIP_ATTN = False
DEBUG_DUMP = False
LAST_RES = None


class Ctr:
    def __init__(self, sem):
        self.sem = sem
        self.n = 0

    def add(self, k=1):
        self.n += k
        return self.n


def _prep(inp):
    g = {k: np.ascontiguousarray(np.asarray(v, dtype=np.float32)) for k, v in inp.items()}
    out = {}

    x = g["L0_attn_x"]
    xs = [np.ascontiguousarray(x[c // 4, (c % 4) * T:(c % 4 + 1) * T, :]) for c in range(NCORES)]

    def cols_from_wT(wT, bias):
        M = wT.shape[1]
        mb = M // 128
        a = wT.reshape(DC, 128, mb, 128).transpose(2, 1, 0, 3)
        bch = np.zeros((mb, 128, 1, 128), np.float32)
        bch[:, 0, 0, :] = bias.reshape(mb, 128)
        return np.ascontiguousarray(
            np.concatenate([a, bch], axis=2).reshape(mb * 128, 17 * 128))

    out["wq_cols"] = cols_from_wT(g["L0_attn_wq"].T, g["L0_attn_bq"])
    out["wo_cols"] = cols_from_wT(g["L0_attn_wo"].T, g["L0_attn_bo"])

    def kvx(w, bias):
        pad = np.zeros((128, D), np.float32)
        pad[0, :] = bias
        return np.ascontiguousarray(np.concatenate([w.T, pad], axis=0))

    def kvtile(a):
        return np.ascontiguousarray(
            a.reshape(17, 128, 8, 256).transpose(2, 1, 0, 3).reshape(8 * 128, 17 * 256))

    out["wkx"] = kvtile(kvx(g["L0_attn_wk"], g["L0_attn_bk"]))
    out["wvx"] = kvtile(kvx(g["L0_attn_wv"], g["L0_attn_bv"]))

    def gu_cols(W, Bflat, sB):
        a = W.reshape(DC, 128, FC, 128).transpose(2, 1, 0, 3)
        comb = np.zeros((FC, 128, 1, 128), np.float32)
        comb[:, 0:64, 0, :] = Bflat.reshape(64, FC, 128).transpose(1, 0, 2)
        comb[:, 64:72, 0, :] = sB.reshape(R, FC, 128).transpose(1, 0, 2)
        return np.ascontiguousarray(
            np.concatenate([a, comb], axis=2).reshape(FC * 128, 17 * 128))

    out["g_cols"] = gu_cols(g["L0_ffn_W_gate"], g["L0_ffn_B_gate"].reshape(64, DFF),
                            g["L0_ffn_sB_gate"])
    out["u_cols"] = gu_cols(g["L0_ffn_W_up"], g["L0_ffn_B_up"].reshape(64, DFF),
                            g["L0_ffn_sB_up"])

    a = g["L0_ffn_W_down"].reshape(FC, 128, DC, 128).transpose(2, 1, 0, 3)
    comb = np.zeros((DC, 128, 1, 128), np.float32)
    comb[:, 0:64, 0, :] = g["L0_ffn_B_down"].reshape(64, DC, 128).transpose(1, 0, 2)
    comb[:, 64:72, 0, :] = g["L0_ffn_sB_down"].reshape(R, DC, 128).transpose(1, 0, 2)
    out["d_cols"] = np.ascontiguousarray(
        np.concatenate([a, comb], axis=2).reshape(DC * 128, 44 * 128))

    def pack(Aexp, sA, n, nb):
        p = np.concatenate([Aexp.transpose(1, 0, 2).reshape(n, 64), sA], axis=1)
        return np.ascontiguousarray(
            p.reshape(nb, 128, PK).transpose(1, 0, 2).reshape(128, nb * PK))

    out["gpack"] = pack(g["L0_ffn_A_gate"], g["L0_ffn_sA_gate"], D, DC)
    out["upack"] = pack(g["L0_ffn_A_up"], g["L0_ffn_sA_up"], D, DC)
    out["dpack"] = pack(g["L0_ffn_A_down"], g["L0_ffn_sA_down"], DFF, FC)

    out["wr_t"] = np.ascontiguousarray(
        g["L0_ffn_W_router"].reshape(DC, 128, E).transpose(1, 0, 2).reshape(128, DC * E))

    out["ident"] = np.eye(128, dtype=np.float32)
    onesx = np.zeros((128, THW), np.float32)
    onesx[0, :] = 1.0
    out["onesx"] = onesx
    pat = np.zeros((E, 64), np.float32)
    for e in range(E):
        pat[e, e * 8:(e + 1) * 8] = 1.0
    out["pattern"] = pat
    out["ebias_b"] = np.ascontiguousarray(np.tile(g["L0_ffn_expert_bias"][None, :], (128, 1)))
    cst = np.zeros((128, 8), np.float32)
    cst[:, 0] = 1e-8
    cst[:, 1] = 1e-5
    cst[:, 2] = -1.0
    cst[:, 3] = -0.5
    out["consts"] = cst
    return xs, out


def build_nc():
    nc = bass.Bass("TRN2", target_bir_lowering=False)

    x_h = nc.dram_tensor("x", [T, D], F32, kind="ExternalInput").ap()
    wq_h = nc.dram_tensor("wq_cols", [DC * 128, 17 * 128], F32, kind="ExternalInput").ap()
    wo_h = nc.dram_tensor("wo_cols", [DC * 128, 17 * 128], F32, kind="ExternalInput").ap()
    wkx_h = nc.dram_tensor("wkx", [8 * 128, 17 * 256], F32, kind="ExternalInput").ap()
    wvx_h = nc.dram_tensor("wvx", [8 * 128, 17 * 256], F32, kind="ExternalInput").ap()
    gcols_h = nc.dram_tensor("g_cols", [FC * 128, 17 * 128], F32, kind="ExternalInput").ap()
    ucols_h = nc.dram_tensor("u_cols", [FC * 128, 17 * 128], F32, kind="ExternalInput").ap()
    dcols_h = nc.dram_tensor("d_cols", [DC * 128, 44 * 128], F32, kind="ExternalInput").ap()
    gpack_h = nc.dram_tensor("gpack", [128, DC * PK], F32, kind="ExternalInput").ap()
    upack_h = nc.dram_tensor("upack", [128, DC * PK], F32, kind="ExternalInput").ap()
    dpack_h = nc.dram_tensor("dpack", [128, FC * PK], F32, kind="ExternalInput").ap()
    wr_h = nc.dram_tensor("wr_t", [128, DC * E], F32, kind="ExternalInput").ap()
    id_h = nc.dram_tensor("ident", [128, 128], F32, kind="ExternalInput").ap()
    onesx_h = nc.dram_tensor("onesx", [128, THW], F32, kind="ExternalInput").ap()
    pat_h = nc.dram_tensor("pattern", [E, 64], F32, kind="ExternalInput").ap()
    ebias_h = nc.dram_tensor("ebias_b", [128, E], F32, kind="ExternalInput").ap()
    cst_h = nc.dram_tensor("consts", [128, 8], F32, kind="ExternalInput").ap()
    out_h = nc.dram_tensor("out", [T, D], F32, kind="ExternalOutput").ap()

    dbg = {}
    if DEBUG_DUMP:
        dbg["d_xn"] = nc.dram_tensor("d_xn", [128, DC * T], BF16, kind="ExternalOutput").ap()
        dbg["d_ktm"] = nc.dram_tensor("d_ktm", [128, NT * D], BF16, kind="ExternalOutput").ap()
        dbg["d_vaug"] = nc.dram_tensor("d_vaug", [128, NT * H * 129], BF16, kind="ExternalOutput").ap()
        dbg["d_scores"] = nc.dram_tensor("d_scores", [128, NT * E], F32, kind="ExternalOutput").ap()
        dbg["d_wtm"] = nc.dram_tensor("d_wtm", [128, NT * E], F32, kind="ExternalOutput").ap()
        dbg["d_wb"] = nc.dram_tensor("d_wb", [64, T], F32, kind="ExternalOutput").ap()
        dbg["d_kvp"] = nc.dram_tensor("d_kvp", [128, H * 129], F32, kind="ExternalOutput").ap()
        dbg["d_kvbf"] = nc.dram_tensor("d_kvbf", [128, H * 129], BF16, kind="ExternalOutput").ap()
        dbg["d_qfm"] = nc.dram_tensor("d_qfm", [128, H * T], BF16, kind="ExternalOutput").ap()
        dbg["d_attn"] = nc.dram_tensor("d_attn", [128, H * T], BF16, kind="ExternalOutput").ap()
        dbg["d_ffn"] = nc.dram_tensor("d_ffn", [D, T], F32, kind="ExternalOutput").ap()
        dbg["d_hid"] = nc.dram_tensor("d_hid", [128, FC * THW], BF16, kind="ExternalOutput").ap()
        dbg["d_xallg"] = nc.dram_tensor("d_xallg", [128, THW], BF16, kind="ExternalOutput").ap()
        dbg["d_xallu"] = nc.dram_tensor("d_xallu", [128, THW], BF16, kind="ExternalOutput").ap()
        dbg["d_xalld"] = nc.dram_tensor("d_xalld", [128, THW], BF16, kind="ExternalOutput").ap()
    cc_in = nc.dram_tensor("cc_in", [H * 128, 129], F32)
    cc_out = nc.dram_tensor("cc_out", [H * 128, 129], F32)
    scratch = nc.dram_tensor("ffn_scratch", [D, T], F32)
    dump3 = nc.dram_tensor("dump3", [T, D], F32)

    st = ExitStack()
    P = {}
    with st:
        def sb(name, shape, dt=F32, stk=None, side=None):
            return (stk or st).enter_context(nc.sbuf_tensor(name, shape, dt, side=side))

        pe = Ctr(st.enter_context(nc.semaphore("pe")))
        dv = Ctr(st.enter_context(nc.semaphore("dv")))
        ac = Ctr(st.enter_context(nc.semaphore("ac")))
        dg = Ctr(st.enter_context(nc.semaphore("dg")))
        cc = Ctr(st.enter_context(nc.semaphore("cc")))
        xld = [Ctr(st.enter_context(nc.semaphore(f"xld{i}"))) for i in range(2)]
        kvs = [Ctr(st.enter_context(nc.semaphore(f"kvs{i}"))) for i in range(2)]
        qcs = [Ctr(st.enter_context(nc.semaphore(f"qcs{i}"))) for i in range(2)]
        kk = Ctr(st.enter_context(nc.semaphore("kk")))
        pc = Ctr(st.enter_context(nc.semaphore("pc")))
        osts = [Ctr(st.enter_context(nc.semaphore(f"ost{i}"))) for i in range(2)]
        ds = xld[0]
        do = cc
        kst = kk
        kbf = kk
        gus = [Ctr(st.enter_context(nc.semaphore(f"gus{i}"))) for i in range(2)]
        dcs = [Ctr(st.enter_context(nc.semaphore(f"dcs{i}"))) for i in range(2)]
        wos = [Ctr(st.enter_context(nc.semaphore(f"wos{i}"))) for i in range(2)]
        fws = [Ctr(st.enter_context(nc.semaphore(f"fws{i}"))) for i in range(2)]
        frs = [Ctr(st.enter_context(nc.semaphore(f"frs{i}"))) for i in range(3)]

        blk = st.enter_context(nc.Block())

        def vpub(v, k=1):
            v.drain().then_inc(dv.sem, k)
            return dv.add(k)

        def apub(sa, k=1):
            sa.drain().then_inc(ac.sem, k)
            return ac.add(k)

        id_bf = sb("id_bf", [128, 128], BF16)
        id_f32 = sb("id_f32", [128, 128], F32)
        onesx_bf = sb("onesx_bf", [128, THW], BF16)
        pat_sb = sb("pat_sb", [E, 64], F32)
        ebias_sb = sb("ebias_sb", [128, E], F32)
        cst_sb = sb("cst_sb", [128, 8], F32)
        gpack_sb = sb("gpack_sb", [128, DC * PK], BF16)
        upack_sb = sb("upack_sb", [128, DC * PK], BF16)
        dpack_sb = sb("dpack_sb", [128, FC * PK], BF16)
        wr_sb = sb("wr_sb", [128, DC * E], BF16)
        scores = sb("scores", [128, NT * E], F32)
        w_tm = sb("w_tm", [128, NT * E], F32)
        w_fm = sb("w_fm", [E, T], F32)
        w_bcast = sb("w_bcast", [64, T], F32)
        kv_part = sb("kv_part", [128, H * 129], F32)
        kv_bf = sb("kv_bf", [128, H * 129], BF16)
        xallg = sb("xallg", [128, THW], BF16)
        xallu = sb("xallu", [128, THW], BF16)
        xalld = sb("xalld", [128, THW], BF16)
        xa_tmp = sb("xa_tmp", [128, THW], F32)
        tk = sb("tk", [128, 16], F32)

        pool = [st.enter_context(nc.psum_tensor(f"ps{i}", [128, 512], F32)) for i in range(8)]

        C_EPS8 = cst_sb[:, 0:1]
        C_EPSLN = cst_sb[:, 1:2]
        C_NEG1 = cst_sb[:, 2:3]
        C_NEGH = cst_sb[:, 3:4]

        @blk.gpsimd
        def _(gp):
            for dst, src in ((id_bf, id_h), (onesx_bf, onesx_h), (wr_sb, wr_h),
                             (gpack_sb, gpack_h), (upack_sb, upack_h), (dpack_sb, dpack_h)):
                gp.dma_start(out=dst[:, :], in_=src).then_inc(dg.sem, 16)
                dg.add(16)
        consts_g = dg.n

        @blk.sync
        def _(s):
            for dst, src in ((id_f32, id_h), (pat_sb, pat_h), (ebias_sb, ebias_h),
                             (cst_sb, cst_h)):
                s.dma_start(out=dst[:, :], in_=src).then_inc(ds.sem, 16)
                ds.add(16)
        consts_s = ds.n

        @blk.vector
        def _(v):
            v.memset(xallg[:, :], 0.0)
            v.memset(xallu[:, :], 0.0)
            v.memset(xalld[:, :], 0.0)
            vpub(v)

        stXN = ExitStack()
        xn_fm = sb("xn_fm", [128, DC * T], BF16, stXN)
        stAB = ExitStack()
        x2 = [sb(f"x2_{i}", [128, D], F32, stAB) for i in range(2)]
        xn_tm = [sb(f"xn_tm{i}", [128, D], BF16, stAB) for i in range(2)]
        stats = sb("stats", [128, 10], F32, stAB)
        k_tm = sb("k_tm", [128, NT * D], BF16, stAB)
        v_aug = sb("v_aug", [128, NT * H * 129], BF16, stAB)
        kvm = {m: [sb(f"kvm_{m}{i}", [128, 17 * 256], BF16, stAB) for i in range(2)]
               for m in ("k", "v")}
        elu_t = [sb(f"elu_t{i}", [128, 512], F32, stAB) for i in range(2)]
        qcol = [sb(f"qcol{i}", [128, 17 * 128], BF16, side="right") for i in range(2)]
        wq_v = wq_h.rearrange("(oc p) m -> oc p m", p=128)

        def emit_qcol_load(oc):
            @blk.gpsimd
            def _(gp, oc=oc):
                if oc < 2:
                    gp.wait_ge(pe.sem, P["a_end_pe"])
                    gp.wait_ge(dv.sem, xn_all)
                    gp.wait_ge(ac.sem, P["a_end_ac"])
                    gp.wait_ge(pc.sem, 128)
                else:
                    gp.wait_ge(pe.sem, P[("qmm", oc - 2)])
                gp.dma_start(out=qcol[oc % 2][:, :], in_=wq_v[oc]).then_inc(qcs[oc % 2].sem, 16)
                P[("qdma", oc)] = qcs[oc % 2].add(16)

        x_v = x_h.rearrange("(tc p) d -> tc p d", p=128)
        xn_view = xn_fm[:, :].rearrange("p (dc t) -> p dc t", t=T)

        for tc in range(NT):
            @blk.sync
            def _(s, tc=tc):
                if tc >= 2:
                    s.wait_ge(dv.sem, P[("xfree", tc - 2)])
                s.dma_start(out=x2[tc % 2][:, :], in_=x_v[tc]).then_inc(xld[tc % 2].sem, 16)
                P[("xdma", tc)] = xld[tc % 2].add(16)

            @blk.vector
            def _(v, tc=tc):
                v.wait_ge(xld[tc % 2].sem, P[("xdma", tc)])
                v.reduce_sum(stats[:, 0:1], x2[tc % 2][:, :], AX.X)
                P[("sum", tc)] = vpub(v)

            @blk.scalar
            def _(sa, tc=tc):
                sa.wait_ge(xld[tc % 2].sem, P[("xdma", tc)])
                sa.wait_ge(ds.sem, consts_s)
                sa.activation(xn_tm[tc % 2][:, :], x2[tc % 2][:, :], AF.Square,
                              accum_out=stats[:, 1:2])
                P[("sq", tc)] = apub(sa)

            @blk.vector
            def _(v, tc=tc):
                v.wait_ge(ac.sem, P[("sq", tc)])
                v.wait_ge(dv.sem, P[("sum", tc)])
                v.tensor_scalar(stats[:, 2:3], stats[:, 0:1], 1.0 / D, None, ALU.mult)
                v.tensor_scalar(stats[:, 3:4], stats[:, 1:2], 1.0 / D, None, ALU.mult)
                v.scalar_tensor_tensor(stats[:, 4:5], stats[:, 0:1], 1.0 / (D * D),
                                       stats[:, 0:1], ALU.mult, ALU.mult)
                n = vpub(v)
                v.wait_ge(dv.sem, n)
                v.tensor_sub(stats[:, 5:6], stats[:, 3:4], stats[:, 4:5])
                P[("var", tc)] = vpub(v)

            @blk.scalar
            def _(sa, tc=tc):
                sa.wait_ge(dv.sem, P[("var", tc)])
                sa.activation(stats[:, 6:7], stats[:, 5:6], AF.Ln, bias=C_EPSLN)
                n = apub(sa)
                sa.wait_ge(ac.sem, n)
                sa.activation(stats[:, 7:8], stats[:, 6:7], AF.Exp, scale=C_NEGH)
                P[("rstd", tc)] = apub(sa)

            @blk.vector
            def _(v, tc=tc):
                v.wait_ge(ac.sem, P[("rstd", tc)])
                v.tensor_scalar(xn_tm[tc % 2][:, :], x2[tc % 2][:, :],
                                stats[:, 2:3], stats[:, 7:8], ALU.subtract, ALU.mult)
                P[("xn", tc)] = P[("xfree", tc)] = vpub(v)

            @blk.sync
            def _(s, tc=tc):
                s.wait_ge(dv.sem, P[("xn", tc)])
                s.dma_start(out=out_h[tc * 128:(tc + 1) * 128, :],
                            in_=x2[tc % 2][:, :]).then_inc(pc.sem, 16)
                P[("pcst", tc)] = pc.add(16)

            for gq in range(4):
                bank = 6 + (gq % 2)

                @blk.tensor
                def _(t, tc=tc, gq=gq, bank=bank):
                    t.wait_ge(dv.sem, P[("xn", tc)])
                    t.wait_ge(dg.sem, consts_g)
                    key = ("trfree", bank)
                    if key in P:
                        t.wait_ge(dv.sem, P[key])
                    for q in range(4):
                        inst = t.matmul(pool[bank][:, q * 128:(q + 1) * 128],
                                        xn_tm[tc % 2][:, (4 * gq + q) * 128:(4 * gq + q + 1) * 128],
                                        id_bf[:, :], start=True, stop=True)
                    inst.then_inc(pe.sem, 1)
                    P[("trmm", tc, gq)] = pe.add(1)

                @blk.vector
                def _(v, tc=tc, gq=gq, bank=bank):
                    v.wait_ge(pe.sem, P[("trmm", tc, gq)])
                    dst = xn_view[:, 4 * gq:4 * gq + 4, tc * 128:(tc + 1) * 128]
                    src = pool[bank][:, :].rearrange("p (q n) -> p q n", n=128)
                    v.tensor_copy(dst, src)
                    P[("trfree", bank)] = vpub(v)
            P[("xnready", tc)] = P[("trfree", 6 + (3 % 2))]
        xn_all = dv.n
        P["a_end_pe"] = pe.n
        P["a_end_ac"] = ac.n
        P["a_end_pe"] = pe.n
        P["a_end_ac"] = ac.n

        wkx_v = wkx_h.rearrange("(oq p) m -> oq p m", p=128)
        wvx_v = wvx_h.rearrange("(oq p) m -> oq p m", p=128)
        NQ = 8

        for oq in range(NQ):
            if oq == 6:
                emit_qcol_load(0)
                emit_qcol_load(1)

            @blk.gpsimd
            def _(gp, oq=oq):
                if oq >= 2:
                    gp.wait_ge(pe.sem, P[("kvbuf", oq - 2)])
                for mv, srcv in (("k", wkx_v), ("v", wvx_v)):
                    gp.dma_start(out=kvm[mv][oq % 2][:, :],
                                 in_=srcv[oq]).then_inc(kvs[oq % 2].sem, 16)
                    kvs[oq % 2].add(16)
                P[("kvdma", oq)] = kvs[oq % 2].n

            for tc in range(NT):
                @blk.tensor
                def _(t, oq=oq, tc=tc):
                    t.wait_ge(kvs[oq % 2].sem, P[("kvdma", oq)])
                    if oq == 0:
                        t.wait_ge(dv.sem, P[("xnready", tc)])
                    for key in (("kbank", tc % 2), ("vbank", tc % 2)):
                        if key in P:
                            t.wait_ge(dv.sem, P[key])
                    if oq == 0 and ("rbank", tc % 2) in P:
                        t.wait_ge(ac.sem, P[("rbank", tc % 2)])
                    for c in range(17):
                        stat = (xn_fm[:, c * T + tc * 128:c * T + (tc + 1) * 128]
                                if c < 16 else onesx_bf[:, 0:128])
                        t.matmul(pool[tc % 2][:, 0:256], stat,
                                 kvm["k"][oq % 2][:, c * 256:(c + 1) * 256],
                                 start=(c == 0), stop=(c == 16), skip_group_check=True)
                        iV = t.matmul(pool[2 + tc % 2][:, 0:256], stat,
                                      kvm["v"][oq % 2][:, c * 256:(c + 1) * 256],
                                      start=(c == 0), stop=(c == 16), skip_group_check=True)
                        if oq == 0 and c < 16:
                            iR = t.matmul(pool[4 + tc % 2][:, 0:E], stat,
                                          wr_sb[:, c * E:(c + 1) * E],
                                          start=(c == 0), stop=(c == 15),
                                          skip_group_check=True)
                            if c == 15:
                                iR.then_inc(pe.sem, 1)
                                P[("rmm", tc)] = pe.add(1)
                    iV.then_inc(pe.sem, 1)
                    P[("kvmm", oq, tc)] = pe.add(1)
                    if tc == NT - 1:
                        P[("kvbuf", oq)] = pe.n

                @blk.scalar
                def _(sa, oq=oq, tc=tc):
                    sa.wait_ge(pe.sem, P[("kvmm", oq, tc)])
                    key = ("kbank", tc % 2)
                    if key in P:
                        sa.wait_ge(dv.sem, P[key])
                    sa.activation(elu_t[tc % 2][:, 0:256], pool[tc % 2][:, 0:256], AF.Exp)
                    P[("kexp", oq, tc)] = apub(sa)

                @blk.vector
                def _(v, oq=oq, tc=tc):
                    v.wait_ge(ac.sem, P[("kexp", oq, tc)])
                    v.tensor_scalar(elu_t[tc % 2][:, 256:512], elu_t[tc % 2][:, 0:256],
                                    1.0, None, ALU.min)
                    n = vpub(v)
                    v.wait_ge(dv.sem, n)
                    v.scalar_tensor_tensor(
                        k_tm[:, tc * D + oq * 256: tc * D + (oq + 1) * 256],
                        pool[tc % 2][:, 0:256], 0.0, elu_t[tc % 2][:, 256:512],
                        ALU.max, ALU.add)
                    P[("kbank", tc % 2)] = vpub(v)
                    va = v_aug[:, :].rearrange("p (tc h e) -> p tc h e", h=H, e=129)
                    v.tensor_copy(va[:, tc, 2 * oq:2 * oq + 2, 0:128],
                                  pool[2 + tc % 2][:, 0:256].rearrange(
                                      "p (h e) -> p h e", e=128))
                    P[("vbank", tc % 2)] = vpub(v)

                if oq == 0:
                    @blk.scalar
                    def _(sa, tc=tc):
                        sa.wait_ge(pe.sem, P[("rmm", tc)])
                        sa.activation(scores[:, tc * E:(tc + 1) * E],
                                      pool[4 + tc % 2][:, 0:E], AF.Sigmoid)
                        P[("sig", tc)] = P[("rbank", tc % 2)] = apub(sa)

        @blk.vector
        def _(v):
            va = v_aug[:, :].rearrange("p (tc h e) -> p tc h e", h=H, e=129)
            v.memset(va[:, :, :, 128:129], 1.0)
            P["vones"] = vpub(v)

        for h in range(H):
            @blk.tensor
            def _(t, h=h):
                t.wait_ge(dv.sem, P["vones"])
                if h >= 2:
                    t.wait_ge(dv.sem, P[("kvevac", h - 2)])
                else:
                    t.wait_ge(ac.sem, P[("sig", NT - 1)])
                t.wait_ge(dv.sem, max(P[("kbank", 0)], P[("kbank", 1)],
                                      P[("vbank", 0)], P[("vbank", 1)]))
                for tc in range(NT):
                    inst = t.matmul(
                        pool[4 + h % 2][:, 0:129],
                        k_tm[:, tc * D + h * 128: tc * D + (h + 1) * 128],
                        v_aug[:, tc * H * 129 + h * 129: tc * H * 129 + (h + 1) * 129],
                        start=(tc == 0), stop=(tc == NT - 1))
                inst.then_inc(pe.sem, 1)
                P[("kvh", h)] = pe.add(1)

            @blk.vector
            def _(v, h=h):
                v.wait_ge(pe.sem, P[("kvh", h)])
                v.tensor_copy(kv_part[:, h * 129:(h + 1) * 129], pool[4 + h % 2][:, 0:129])
                P[("kvevac", h)] = vpub(v)

        @blk.sync
        def _(s):
            s.wait_ge(dv.sem, P[("kvevac", H - 1)])
            dst = cc_in.ap().rearrange("(h d) e -> d h e", d=128)
            src = kv_part[:, :].rearrange("p (h e) -> p h e", e=129)
            s.dma_start(out=dst[:, 0:8, :], in_=src[:, 0:8, :]).then_inc(kst.sem, 16)
            s.dma_start(out=dst[:, 8:16, :], in_=src[:, 8:16, :]).then_inc(kst.sem, 16)
            P["kvstore"] = kst.add(32)

        if DEBUG_DUMP:
            @blk.sync
            def _(s):
                s.wait_ge(dv.sem, dv.n)
                s.wait_ge(ac.sem, ac.n)
                s.dma_start(out=dbg["d_xn"], in_=xn_fm[:, :]).then_inc(do.sem, 16)
                s.dma_start(out=dbg["d_ktm"], in_=k_tm[:, :]).then_inc(do.sem, 16)
                s.dma_start(out=dbg["d_vaug"], in_=v_aug[:, :]).then_inc(do.sem, 16)
                s.dma_start(out=dbg["d_kvp"], in_=kv_part[:, :]).then_inc(do.sem, 16)
                s.dma_start(out=dbg["d_scores"], in_=scores[:, :]).then_inc(do.sem, 16)
                do.add(5 * 16)
                s.wait_ge(do.sem, do.n)
            P["dbg1"] = do.n
        b_end_pe = pe.n
        b_end_dv = dv.n
        stAB.close()

        for tc in range(NT):
            @blk.vector
            def _(v, tc=tc):
                sc = scores[:, tc * E:(tc + 1) * E]
                v.wait_ge(ac.sem, P[("sig", tc)])
                v.tensor_add(tk[:, 0:8], sc, ebias_sb[:, :])
                n = vpub(v)
                v.wait_ge(dv.sem, n)
                v.reduce_max(tk[:, 8:9], tk[:, 0:8], AX.X)
                n = vpub(v)
                v.wait_ge(dv.sem, n)
                v.tensor_scalar(tk[:, 0:8], tk[:, 0:8], tk[:, 8:9], None, ALU.is_equal)
                P[("eq", tc)] = vpub(v)

            @blk.vector
            def _(v, tc=tc):
                sc = scores[:, tc * E:(tc + 1) * E]
                v.wait_ge(dv.sem, P[("eq", tc)])
                v.tensor_add(xa_tmp[:, 0:8], sc, ebias_sb[:, :])
                n = vpub(v)
                v.wait_ge(dv.sem, n)
                v.scalar_tensor_tensor(xa_tmp[:, 8:16], tk[:, 0:8], -1e30,
                                       xa_tmp[:, 0:8], ALU.mult, ALU.add)
                n = vpub(v)
                v.wait_ge(dv.sem, n)
                v.reduce_max(tk[:, 8:9], xa_tmp[:, 8:16], AX.X)
                n = vpub(v)
                v.wait_ge(dv.sem, n)
                v.tensor_scalar(xa_tmp[:, 0:8], xa_tmp[:, 0:8], tk[:, 8:9], None,
                                ALU.is_ge)
                n = vpub(v)
                v.wait_ge(dv.sem, n)
                v.tensor_mul(w_tm[:, tc * E:(tc + 1) * E], sc, xa_tmp[:, 0:8])
                n = vpub(v)
                v.wait_ge(dv.sem, n)
                v.reduce_sum(tk[:, 8:9], w_tm[:, tc * E:(tc + 1) * E], AX.X)
                P[("den", tc)] = vpub(v)

            @blk.scalar
            def _(sa, tc=tc):
                sa.wait_ge(dv.sem, P[("den", tc)])
                sa.activation(tk[:, 10:11], tk[:, 8:9], AF.Ln, bias=C_EPS8)
                n = apub(sa)
                sa.wait_ge(ac.sem, n)
                sa.activation(tk[:, 11:12], tk[:, 10:11], AF.Exp, scale=C_NEG1)
                P[("denr", tc)] = apub(sa)

            @blk.vector
            def _(v, tc=tc):
                v.wait_ge(ac.sem, P[("denr", tc)])
                v.tensor_scalar(w_tm[:, tc * E:(tc + 1) * E], w_tm[:, tc * E:(tc + 1) * E],
                                tk[:, 11:12], None, ALU.mult)
                P[("wtm", tc)] = vpub(v)

        stQ = ExitStack()
        q_fm = sb("q_fm", [128, H * T], BF16, stQ, side="right")
        stC = ExitStack()
        qt = [sb("qt0", [128, 512], F32, stC)] * 2

        for oc in range(H):
            if oc >= 2:
                emit_qcol_load(oc)

            @blk.tensor
            def _(t, oc=oc):
                t.wait_ge(qcs[oc % 2].sem, P[("qdma", oc)])
                for b in range(2):
                    key = ("qbank", (oc % 2) * 2 + b)
                    if key in P:
                        t.wait_ge(dv.sem, P[key])
                    elif oc < 2:
                        t.wait_ge(dv.sem, b_end_dv)
                for c in range(17):
                    for th in range(TH):
                        bank = (oc % 2) * 2 + th
                        mov = (xn_fm[:, c * T + th * THW: c * T + (th + 1) * THW]
                               if c < 16 else onesx_bf[:, :])
                        inst = t.matmul(pool[bank][:, :],
                                        qcol[oc % 2][:, c * 128:(c + 1) * 128],
                                        mov, start=(c == 0), stop=(c == 16),
                                        skip_group_check=True)
                inst.then_inc(pe.sem, 1)
                P[("qmm", oc)] = pe.add(1)

            for th in range(TH):
                bank = (oc % 2) * 2 + th

                @blk.scalar
                def _(sa, oc=oc, th=th, bank=bank):
                    sa.wait_ge(pe.sem, P[("qmm", oc)])
                    key = "qtmp"
                    if key in P:
                        sa.wait_ge(dv.sem, P[key])
                    sa.activation(qt[th][:, :], pool[bank][:, :], AF.Exp)
                    P[("qexp", oc, th)] = apub(sa)

                @blk.vector
                def _(v, oc=oc, th=th, bank=bank):
                    v.wait_ge(ac.sem, P[("qexp", oc, th)])
                    v.tensor_scalar(qt[th][:, :], qt[th][:, :], 1.0, None, ALU.min)
                    n = vpub(v)
                    v.wait_ge(dv.sem, n)
                    v.scalar_tensor_tensor(
                        q_fm[:, oc * T + th * THW: oc * T + (th + 1) * THW],
                        pool[bank][:, :], 0.0, qt[th][:, :], ALU.max, ALU.add)
                    P[("qbank", bank)] = P["qtmp"] = vpub(v)
        for tc in range(NT):
            @blk.tensor
            def _(t, tc=tc):
                t.wait_ge(dv.sem, P[("wtm", tc)])
                t.wait_ge(ds.sem, consts_s)
                if tc >= 2:
                    t.wait_ge(dv.sem, P[("wfm", tc - 2)])
                else:
                    t.wait_ge(dv.sem, max(P[("qbank", 2)], P[("qbank", 3)]))
                inst = t.matmul(pool[2 + tc % 2][0:E, 0:128],
                                w_tm[:, tc * E:(tc + 1) * E], id_f32[:, :],
                                start=True, stop=True)
                inst.then_inc(pe.sem, 1)
                P[("wtr", tc)] = pe.add(1)

            @blk.vector
            def _(v, tc=tc):
                v.wait_ge(pe.sem, P[("wtr", tc)])
                v.tensor_copy(w_fm[:, tc * 128:(tc + 1) * 128], pool[2 + tc % 2][0:E, 0:128])
                P[("wfm", tc)] = vpub(v)

        for half in range(2):
            @blk.tensor
            def _(t, half=half):
                t.wait_ge(dv.sem, P[("wfm", NT - 1)])
                inst = t.matmul(pool[2 + half][0:64, 0:512], pat_sb[:, :],
                                w_fm[:, half * 512:(half + 1) * 512],
                                start=True, stop=True)
                inst.then_inc(pe.sem, 1)
                P[("wbc_mm", half)] = pe.add(1)

            @blk.vector
            def _(v, half=half):
                v.wait_ge(pe.sem, P[("wbc_mm", half)])
                v.tensor_copy(w_bcast[:, half * 512:(half + 1) * 512],
                              pool[2 + half][0:64, 0:512])
                P[("wbc", half)] = vpub(v)

        P["qdone_pe"] = pe.n
        c_end_dv = dv.n

        @blk.gpsimd
        def _(gp):
            gp.wait_ge(kst.sem, P["kvstore"])
            gp.collective_compute(
                "AllReduce", ALU.add,
                replica_groups=[[0, 1, 2, 3], [4, 5, 6, 7]],
                ins=[cc_in.ap().opt()], outs=[cc_out.ap().opt()],
            ).then_inc(cc.sem)
            cc.add(1)

        def emit_kv_readback():
            @blk.gpsimd
            def _(gp):
                gp.wait_ge(cc.sem, 1)
                dst = kv_bf[:, :].rearrange("p (h e) -> p h e", e=129)
                srcv = cc_out.ap().rearrange("(h d) e -> d h e", d=128)
                gp.dma_start(out=dst[:, 0:8, :], in_=srcv[:, 0:8, :]).then_inc(kbf.sem, 16)
                gp.dma_start(out=dst[:, 8:16, :], in_=srcv[:, 8:16, :]).then_inc(kbf.sem, 16)
                P["kvbf"] = kbf.add(32)

        if DEBUG_DUMP:
            @blk.sync
            def _(s):
                s.wait_ge(dv.sem, dv.n)
                s.wait_ge(kbf.sem, P["kvbf"])
                s.dma_start(out=dbg["d_qfm"], in_=q_fm[:, :]).then_inc(do.sem, 16)
                s.dma_start(out=dbg["d_wtm"], in_=w_tm[:, :]).then_inc(do.sem, 16)
                s.dma_start(out=dbg["d_wb"], in_=w_bcast[:, :]).then_inc(do.sem, 16)
                s.dma_start(out=dbg["d_kvbf"], in_=kv_bf[:, :]).then_inc(do.sem, 16)
                do.add(4 * 16)
                s.wait_ge(do.sem, do.n)

        stE = ExitStack()
        gcol = [sb(f"gcol{i}", [128, 17 * 128], BF16, stE) for i in range(2)]
        ucol = [sb(f"ucol{i}", [128, 17 * 128], BF16, stE) for i in range(2)]
        dcol = [sb(f"dcol{i}", [128, 44 * 128], BF16, stE) for i in range(2)]
        hid = sb("hid", [128, FC * THW], BF16, stE)
        silu_t = [sb(f"silu{i}", [128, THW], F32, stE) for i in range(2)]
        fstage = [sb(f"fstage{i}", [128, THW], F32, stE) for i in range(2)]
        gcols_v = gcols_h.rearrange("(fc p) m -> fc p m", p=128)
        ucols_v = ucols_h.rearrange("(fc p) m -> fc p m", p=128)
        dcols_v = dcols_h.rearrange("(dc p) m -> dc p m", p=128)

        if not SKIP_FFN:
            for th in range(TH):
                for nm, pk_sb, bank in (("g", gpack_sb, 4), ("u", upack_sb, 5)):
                    @blk.tensor
                    def _(t, th=th, nm=nm, pk_sb=pk_sb, bank=bank):
                        if th == 0 and nm == "g":
                            t.wait_ge(pe.sem, P["qdone_pe"])
                            t.wait_ge(dv.sem, c_end_dv)
                        if th == 1:
                            t.wait_ge(dv.sem, P[("xa_ev", nm, 0)])
                        for c in range(DC):
                            inst = t.matmul(pool[bank][0:PK, 0:THW],
                                            pk_sb[:, c * PK:(c + 1) * PK],
                                            xn_fm[:, c * T + th * THW: c * T + (th + 1) * THW],
                                            start=(c == 0), stop=(c == DC - 1))
                        inst.then_inc(pe.sem, 1)
                        P[("xamm", nm, th)] = pe.add(1)

                    @blk.vector
                    def _(v, th=th, nm=nm, bank=bank):
                        xall = xallg if nm == "g" else xallu
                        v.wait_ge(pe.sem, P[("xamm", nm, th)])
                        v.wait_ge(dv.sem, max(P[("wbc", 0)], P[("wbc", 1)]))
                        if th == 1:
                            v.wait_ge(pe.sem, P[("gumm", 42, th - 1)])
                        v.tensor_scalar(xa_tmp[0:64, 0:THW], pool[bank][0:64, 0:THW],
                                        0.125, None, ALU.mult)
                        n = vpub(v)
                        v.wait_ge(dv.sem, n)
                        v.tensor_mul(xall[0:64, :], xa_tmp[0:64, 0:THW],
                                     w_bcast[0:64, th * THW:(th + 1) * THW])
                        n = vpub(v)
                        v.tensor_scalar(xall[64:PK, :], pool[bank][64:PK, 0:THW],
                                        0.125, None, ALU.mult)
                        P[("xa_ev", nm, th)] = vpub(v)

                for fc in range(FC):
                    @blk.gpsimd
                    def _(gp, th=th, fc=fc):
                        idx = th * FC + fc
                        if idx >= 2:
                            pth, pfc = divmod(idx - 2, FC)
                            gp.wait_ge(pe.sem, P[("gumm", pfc, pth)])
                        if idx < 2:
                            gp.wait_ge(pe.sem, b_end_pe)
                        gp.dma_start(out=gcol[fc % 2][:, :], in_=gcols_v[fc]).then_inc(gus[fc % 2].sem, 16)
                        gp.dma_start(out=ucol[fc % 2][:, :], in_=ucols_v[fc]).then_inc(gus[fc % 2].sem, 16)
                        P[("gudma", fc, th)] = gus[fc % 2].add(32)

                    @blk.tensor
                    def _(t, th=th, fc=fc):
                        t.wait_ge(gus[fc % 2].sem, P[("gudma", fc, th)])
                        t.wait_ge(dv.sem, P[("xa_ev", "g", th)])
                        t.wait_ge(dv.sem, P[("xa_ev", "u", th)])
                        kk = ("gubank", fc % 2)
                        if kk in P:
                            t.wait_ge(dv.sem, P[kk])
                        for c in range(17):
                            movg = (xn_fm[:, c * T + th * THW: c * T + (th + 1) * THW]
                                    if c < 16 else xallg[:, :])
                            movu = (xn_fm[:, c * T + th * THW: c * T + (th + 1) * THW]
                                    if c < 16 else xallu[:, :])
                            t.matmul(pool[fc % 2][:, :], gcol[fc % 2][:, c * 128:(c + 1) * 128],
                                     movg, start=(c == 0), stop=(c == 16), skip_group_check=True)
                            iU = t.matmul(pool[2 + fc % 2][:, :],
                                          ucol[fc % 2][:, c * 128:(c + 1) * 128],
                                          movu, start=(c == 0), stop=(c == 16),
                                          skip_group_check=True)
                        if fc >= 2:
                            j = fc - 2
                            t.matmul(pool[4][0:PK, 0:THW],
                                     dpack_sb[:, j * PK:(j + 1) * PK],
                                     hid[:, j * THW:(j + 1) * THW],
                                     start=(j == 0), stop=False, skip_group_check=True)
                        iU.then_inc(pe.sem, 1)
                        P[("gumm", fc, th)] = pe.add(1)

                    @blk.scalar
                    def _(sa, th=th, fc=fc):
                        sa.wait_ge(pe.sem, P[("gumm", fc, th)])
                        key = ("gubank", fc % 2)
                        if key in P:
                            sa.wait_ge(dv.sem, P[key])
                        sa.activation(silu_t[fc % 2][:, :], pool[fc % 2][:, :], AF.Silu)
                        P[("silu", fc, th)] = apub(sa)

                    @blk.vector
                    def _(v, th=th, fc=fc):
                        v.wait_ge(ac.sem, P[("silu", fc, th)])
                        v.tensor_mul(hid[:, fc * THW:(fc + 1) * THW],
                                     silu_t[fc % 2][:, :], pool[2 + fc % 2][:, :])
                        P[("gubank", fc % 2)] = P[("hidp", fc, th)] = vpub(v)

                @blk.tensor
                def _(t, th=th):
                    for j in (FC - 2, FC - 1):
                        t.wait_ge(dv.sem, P[("hidp", j, th)])
                        inst = t.matmul(pool[4][0:PK, 0:THW],
                                        dpack_sb[:, j * PK:(j + 1) * PK],
                                        hid[:, j * THW:(j + 1) * THW],
                                        start=False, stop=(j == FC - 1),
                                        skip_group_check=True)
                    inst.then_inc(pe.sem, 1)
                    P[("xadmm", th)] = pe.add(1)

                @blk.vector
                def _(v, th=th):
                    v.wait_ge(pe.sem, P[("xadmm", th)])
                    v.tensor_scalar(xa_tmp[0:64, 0:THW], pool[4][0:64, 0:THW],
                                    0.125, None, ALU.mult)
                    n = vpub(v)
                    v.wait_ge(dv.sem, n)
                    v.tensor_mul(xalld[0:64, :], xa_tmp[0:64, 0:THW],
                                 w_bcast[0:64, th * THW:(th + 1) * THW])
                    n = vpub(v)
                    v.tensor_scalar(xalld[64:PK, :], pool[4][64:PK, 0:THW],
                                    0.125, None, ALU.mult)
                    P[("xad_ev", th)] = vpub(v)

                if DEBUG_DUMP and th == 0:
                    @blk.sync
                    def _(s):
                        s.wait_ge(dv.sem, P[("xad_ev", 0)])
                        s.dma_start(out=dbg["d_hid"], in_=hid[:, :]).then_inc(do.sem, 16)
                        s.dma_start(out=dbg["d_xallg"], in_=xallg[:, :]).then_inc(do.sem, 16)
                        s.dma_start(out=dbg["d_xallu"], in_=xallu[:, :]).then_inc(do.sem, 16)
                        s.dma_start(out=dbg["d_xalld"], in_=xalld[:, :]).then_inc(do.sem, 16)
                        do.add(4 * 16)
                        s.wait_ge(do.sem, do.n)

                for dc in range(DC):
                    @blk.gpsimd
                    def _(gp, th=th, dc=dc):
                        idx = th * DC + dc
                        if idx >= 2:
                            pth, pdc = divmod(idx - 2, DC)
                            gp.wait_ge(pe.sem, P[("dmm", pdc, pth)])
                        if idx == 0:
                            gp.wait_ge(pe.sem, P[("gumm", FC - 1, 0)])
                        gp.dma_start(out=dcol[dc % 2][:, :], in_=dcols_v[dc]).then_inc(dcs[dc % 2].sem, 16)
                        P[("ddma", dc, th)] = dcs[dc % 2].add(16)

                    @blk.tensor
                    def _(t, th=th, dc=dc):
                        t.wait_ge(dcs[dc % 2].sem, P[("ddma", dc, th)])
                        t.wait_ge(dv.sem, P[("xad_ev", th)])
                        key = ("dbank", dc % 2)
                        if key in P:
                            t.wait_ge(dv.sem, P[key])
                        for c in range(44):
                            mov = (hid[:, c * THW:(c + 1) * THW] if c < FC else xalld[:, :])
                            inst = t.matmul(pool[dc % 2][:, :],
                                            dcol[dc % 2][:, c * 128:(c + 1) * 128],
                                            mov, start=(c == 0), stop=(c == 43))
                        inst.then_inc(pe.sem, 1)
                        P[("dmm", dc, th)] = pe.add(1)

                    @blk.vector
                    def _(v, th=th, dc=dc):
                        v.wait_ge(pe.sem, P[("dmm", dc, th)])
                        if ("fst", dc % 2) in P:
                            v.wait_ge(fws[dc % 2].sem, P[("fst", dc % 2)])
                        v.tensor_copy(fstage[dc % 2][:, :], pool[dc % 2][:, :])
                        P[("dbank", dc % 2)] = P[("fev", dc, th)] = vpub(v)

                    @blk.sync
                    def _(s, th=th, dc=dc):
                        s.wait_ge(dv.sem, P[("fev", dc, th)])
                        s.dma_start(out=scratch.ap()[dc * 128:(dc + 1) * 128,
                                                     th * THW:(th + 1) * THW],
                                    in_=fstage[dc % 2][:, :]).then_inc(fws[dc % 2].sem, 16)
                        P[("fst", dc % 2)] = P[("fsc", dc, th)] = fws[dc % 2].add(16)
                if th == 0:
                    emit_kv_readback()
        e_end_pe = pe.n
        e_end_dv = dv.n
        stE.close()
        stC.close()
        stXN.close()

        stD = ExitStack()
        attn_fm = sb("attn_fm", [128, H * T], BF16, stD)
        t1row = [sb(f"t1row{i}", [1, THW], F32, stD) for i in range(2)]
        rrow = [sb(f"rrow{i}", [1, THW], BF16, stD) for i in range(2)]
        bc_sb = [sb(f"bc{i}", [128, THW], F32, stD) for i in range(2)]
        wocol = [sb(f"wocol{i}", [128, 17 * 128], BF16, stD) for i in range(2)]
        wocol32 = [sb(f"wocol32_{i}", [128, 17 * 128], F32, stD) for i in range(2)]
        s1 = [sb(f"s1_{i}", [128, THW], F32, stD) for i in range(2)]
        fbuf = [sb(f"fbuf{i}", [128, THW], F32, stD) for i in range(3)]
        ostage = sb("ostage", [128, 4 * D], F32, stD)
        wo_v = wo_h.rearrange("(dc p) m -> dc p m", p=128)
        out_q = out_h.rearrange("(q p) d -> p q d", p=128)

        def emit_bc(itb):
            thb, hb = divmod(itb, H)

            @blk.tensor
            def _(t, itb=itb):
                t.wait_ge(ac.sem, P[("rr", itb)])
                if ("bcfree", itb % 2) in P:
                    t.wait_ge(dv.sem, P[("bcfree", itb % 2)])
                inst = t.matmul(pool[6 + itb % 2][:, :],
                                onesx_bf[0:1, 0:128], rrow[itb % 2][:, :],
                                start=True, stop=True)
                inst.then_inc(pe.sem, 1)
                P[("bcmm", itb)] = P[("rrfree", itb % 2)] = pe.add(1)

            @blk.vector
            def _(v, itb=itb, thb=thb, hb=hb):
                v.wait_ge(pe.sem, P[("bcmm", itb)])
                v.tensor_copy(bc_sb[itb % 2][:, :], pool[6 + itb % 2][:, :])
                P[("bcfree", itb % 2)] = n = vpub(v)
                v.wait_ge(dv.sem, n)
                v.tensor_mul(attn_fm[:, hb * T + thb * THW: hb * T + (thb + 1) * THW],
                             pool[2 + itb % 2][:, :], bc_sb[itb % 2][:, :])
                P[("qkvfree", itb % 2)] = vpub(v)

        for th in range(TH):
            for h in range(H):
                it = th * H + h

                @blk.tensor
                def _(t, th=th, h=h, it=it):
                    t.wait_ge(kbf.sem, P["kvbf"])
                    if it == 0:
                        t.wait_ge(pe.sem, e_end_pe)
                        t.wait_ge(dv.sem, e_end_dv)
                        t.wait_ge(ac.sem, ac.n)
                    if ("qkvfree", it % 2) in P:
                        t.wait_ge(dv.sem, P[("qkvfree", it % 2)])
                    if ("denfree", it % 2) in P:
                        t.wait_ge(ac.sem, P[("denfree", it % 2)])
                    mov = q_fm[:, h * T + th * THW: h * T + (th + 1) * THW]
                    t.matmul(pool[2 + it % 2][:, :], kv_bf[:, h * 129:h * 129 + 128],
                             mov, start=True, stop=True)
                    inst = t.matmul(pool[4 + it % 2][0:1, :],
                                    kv_bf[:, h * 129 + 128:h * 129 + 129],
                                    mov, start=True, stop=True)
                    inst.then_inc(pe.sem, 1)
                    P[("qkvmm", it)] = pe.add(1)

                @blk.scalar
                def _(sa, it=it):
                    sa.wait_ge(pe.sem, P[("qkvmm", it)])
                    if ("rrfree", it % 2) in P:
                        sa.wait_ge(pe.sem, P[("rrfree", it % 2)])
                    sa.activation(t1row[it % 2][:, :], pool[4 + it % 2][0:1, :],
                                  AF.Ln, bias=cst_sb[0:1, 0:1])
                    P[("denfree", it % 2)] = n = apub(sa)
                    sa.wait_ge(ac.sem, n)
                    sa.activation(rrow[it % 2][:, :], t1row[it % 2][:, :],
                                  AF.Exp, scale=cst_sb[0:1, 2:3])
                    P[("rr", it)] = apub(sa)

                if it >= 1:
                    emit_bc(it - 1)
                if it == TH * H - 1:
                    emit_bc(it)
        attn_ready = dv.n
        if DEBUG_DUMP:
            @blk.sync
            def _(s):
                s.wait_ge(dv.sem, attn_ready)
                s.dma_start(out=dbg["d_attn"], in_=attn_fm[:, :]).then_inc(do.sem, 16)
                if not SKIP_FFN:
                    s.wait_ge(ds.sem, P[("fsc", DC - 1, TH - 1)])
                    s.dma_start(out=dbg["d_ffn"], in_=scratch.ap()).then_inc(do.sem, 16)
                    do.add(16)
                do.add(16)
                s.wait_ge(do.sem, do.n)

        def emit_otr(jdx):
            jth, jdc = divmod(jdx, DC)

            @blk.tensor
            def _(t, jdx=jdx, jth=jth, jdc=jdc):
                t.wait_ge(dv.sem, P[("s1p", jdc, jth)])
                key = ("otrfree", jdx % 2)
                if key in P:
                    t.wait_ge(dv.sem, P[key])
                for q in range(4):
                    inst = t.transpose(pool[6 + jdx % 2][:, q * 128:(q + 1) * 128],
                                       s1[jdx % 2][:, q * 128:(q + 1) * 128],
                                       id_f32[:, :])
                inst.then_inc(pe.sem, 1)
                P[("otr", jdc, jth)] = P[("s1free", jdx % 2)] = pe.add(1)

            @blk.vector
            def _(v, jdx=jdx, jth=jth, jdc=jdc):
                v.wait_ge(pe.sem, P[("otr", jdc, jth)])
                if jth == 1 and jdc == 0:
                    v.wait_ge(osts[0].sem, 32)
                dst = ostage[:, :].rearrange("p (q d) -> p q d", d=D)[
                    :, :, jdc * 128:(jdc + 1) * 128]
                v.tensor_copy(dst, pool[6 + jdx % 2][:, :].rearrange(
                    "p (q n) -> p q n", n=128))
                P[("otrfree", jdx % 2)] = P[("oev", jdc, jth)] = vpub(v)

            if jdc % 8 == 7:
                kq = jdc // 8
                @blk.gpsimd
                def _(gp, jth=jth, kq=kq):
                    gp.wait_ge(dv.sem, P[("oev", 8 * kq + 7, jth)])
                    gp.wait_ge(pc.sem, 128)
                    dstq = out_h.rearrange("(q p) (kq n) -> p q kq n", p=128, n=1024)[
                        :, jth * 4:(jth + 1) * 4, kq, :]
                    srcq = ostage[:, :].rearrange("p (q kq n) -> p q kq n",
                                                  kq=2, n=1024)[:, :, kq, :]
                    gp.dma_start(out=dstq, in_=srcq,
                                 accum_op=ALU.add).then_inc(osts[jth].sem, 16)
                    P[("oacc", jth)] = osts[jth].add(16)

        for th in range(TH):
            for dc in range(DC):
                idx = th * DC + dc

                @blk.sync
                def _(s, th=th, dc=dc, idx=idx):
                    if idx == 0:
                        s.wait_ge(dv.sem, attn_ready)
                    if idx < 2:
                        s.wait_ge(pe.sem, e_end_pe)
                    if idx >= 2:
                        s.wait_ge(ac.sem, P[("wocast", idx - 2)])
                    s.dma_start(out=wocol32[idx % 2][:, :], in_=wo_v[dc]
                                ).then_inc(wos[idx % 2].sem, 16)
                    P[("wodma", dc, th)] = wos[idx % 2].add(16)

                @blk.scalar
                def _(sa, th=th, dc=dc, idx=idx):
                    sa.wait_ge(wos[idx % 2].sem, P[("wodma", dc, th)])
                    if idx >= 2:
                        pth, pdc = divmod(idx - 2, DC)
                        sa.wait_ge(pe.sem, P[("womm", pdc, pth)])
                    sa.activation(wocol[idx % 2][:, :], wocol32[idx % 2][:, :], AF.Identity)
                    P[("wocast", idx)] = apub(sa)

                @blk.sync
                def _(s, th=th, dc=dc, idx=idx):
                    fb = idx % 3
                    if idx < 3:
                        s.wait_ge(pe.sem, e_end_pe)
                    if not SKIP_FFN:
                        s.wait_ge(fws[dc % 2].sem, P[("fsc", dc, th)])
                    if idx >= 3:
                        pth, pdc = divmod(idx - 3, DC)
                        s.wait_ge(dv.sem, P[("s1p", pdc, pth)])
                    s.dma_start(out=fbuf[fb][:, :],
                                in_=scratch.ap()[dc * 128:(dc + 1) * 128,
                                                 th * THW:(th + 1) * THW]
                                ).then_inc(frs[fb].sem, 16)
                    P[("fread", dc, th)] = frs[fb].add(16)

                @blk.tensor
                def _(t, th=th, dc=dc, idx=idx):
                    t.wait_ge(ac.sem, P[("wocast", idx)])
                    if idx == 0:
                        t.wait_ge(dv.sem, attn_ready)
                    key = ("wofree", idx % 2)
                    if key in P:
                        t.wait_ge(dv.sem, P[key])
                    for c in range(17):
                        mov = (attn_fm[:, c * T + th * THW: c * T + (th + 1) * THW]
                               if c < 16 else onesx_bf[:, :])
                        inst = t.matmul(pool[idx % 2][:, :],
                                        wocol[idx % 2][:, c * 128:(c + 1) * 128],
                                        mov, start=(c == 0), stop=(c == 16))
                    inst.then_inc(pe.sem, 1)
                    P[("womm", dc, th)] = pe.add(1)

                @blk.vector
                def _(v, th=th, dc=dc, idx=idx):
                    v.wait_ge(pe.sem, P[("womm", dc, th)])
                    v.wait_ge(frs[idx % 3].sem, P[("fread", dc, th)])
                    key = ("s1free", idx % 2)
                    if key in P:
                        v.wait_ge(pe.sem, P[key])
                    v.tensor_add(s1[idx % 2][:, :], pool[idx % 2][:, :], fbuf[idx % 3][:, :])
                    P[("wofree", idx % 2)] = P[("s1p", dc, th)] = vpub(v)

                if idx >= 1:
                    emit_otr(idx - 1)
                if idx == TH * DC - 1:
                    emit_otr(idx)

        @blk.sync
        def _(s):
            for og in range(2):
                s.wait_ge(osts[og].sem, osts[og].n)

        stD.close()
        stQ.close()
    return nc


_CACHE = {}


def kernel(**inputs):
    global LAST_RES
    xs, shared = _prep(inputs)
    key = ("nc", SKIP_FFN, SKIP_ATTN, DEBUG_DUMP)
    if key not in _CACHE:
        _CACHE[key] = build_nc()
    nc = _CACHE[key]
    in_maps = []
    for c in range(NCORES):
        m = {"x": xs[c]}
        m.update(shared)
        in_maps.append(m)
    res = run_bass_kernel_spmd(nc, in_maps, core_ids=list(range(NCORES)))
    LAST_RES = res
    out = np.empty((2, 4 * T, D), np.float32)
    for c in range(NCORES):
        out[c // 4, (c % 4) * T:(c % 4 + 1) * T, :] = res.results[c]["out"]
    return out


# version 42 (final)
# speedup vs baseline: 1.0336x; 1.0076x over previous
import numpy as np
from contextlib import ExitStack

import concourse.bass as bass
import concourse.mybir as mybir
from concourse.bass_utils import run_bass_kernel_spmd

F32 = mybir.dt.float32
BF16 = mybir.dt.bfloat16
AF = mybir.ActivationFunctionType
ALU = mybir.AluOpType
AX = mybir.AxisListType

D, DC = 2048, 16
DFF, FC = 5504, 43
T, NT = 1024, 8
TH, THW = 2, 512
H = 16
E, R, PK = 8, 8, 72
NCORES = 8

SKIP_FFN = False
SKIP_ATTN = False
DEBUG_DUMP = False
LAST_RES = None


class Ctr:
    def __init__(self, sem):
        self.sem = sem
        self.n = 0

    def add(self, k=1):
        self.n += k
        return self.n


def _prep(inp):
    g = {k: np.ascontiguousarray(np.asarray(v, dtype=np.float32)) for k, v in inp.items()}
    out = {}

    x = g["L0_attn_x"]
    xs = [np.ascontiguousarray(x[c // 4, (c % 4) * T:(c % 4 + 1) * T, :]) for c in range(NCORES)]

    def cols_from_wT(wT, bias):
        M = wT.shape[1]
        mb = M // 128
        a = wT.reshape(DC, 128, mb, 128).transpose(2, 1, 0, 3)
        bch = np.zeros((mb, 128, 1, 128), np.float32)
        bch[:, 0, 0, :] = bias.reshape(mb, 128)
        return np.ascontiguousarray(
            np.concatenate([a, bch], axis=2).reshape(mb * 128, 17 * 128))

    out["wq_cols"] = cols_from_wT(g["L0_attn_wq"].T, g["L0_attn_bq"])
    out["wo_cols"] = cols_from_wT(g["L0_attn_wo"].T, g["L0_attn_bo"])

    def kvx(w, bias):
        pad = np.zeros((128, D), np.float32)
        pad[0, :] = bias
        return np.ascontiguousarray(np.concatenate([w.T, pad], axis=0))

    def kvtile(a):
        return np.ascontiguousarray(
            a.reshape(17, 128, 8, 256).transpose(2, 1, 0, 3).reshape(8 * 128, 17 * 256))

    out["wkx"] = kvtile(kvx(g["L0_attn_wk"], g["L0_attn_bk"]))
    out["wvx"] = kvtile(kvx(g["L0_attn_wv"], g["L0_attn_bv"]))

    def gu_cols(W, Bflat, sB):
        a = W.reshape(DC, 128, FC, 128).transpose(2, 1, 0, 3)
        comb = np.zeros((FC, 128, 1, 128), np.float32)
        comb[:, 0:64, 0, :] = Bflat.reshape(64, FC, 128).transpose(1, 0, 2)
        comb[:, 64:72, 0, :] = sB.reshape(R, FC, 128).transpose(1, 0, 2)
        return np.ascontiguousarray(
            np.concatenate([a, comb], axis=2).reshape(FC * 128, 17 * 128))

    out["g_cols"] = gu_cols(g["L0_ffn_W_gate"], g["L0_ffn_B_gate"].reshape(64, DFF),
                            g["L0_ffn_sB_gate"])
    out["u_cols"] = gu_cols(g["L0_ffn_W_up"], g["L0_ffn_B_up"].reshape(64, DFF),
                            g["L0_ffn_sB_up"])

    a = g["L0_ffn_W_down"].reshape(FC, 128, DC, 128).transpose(2, 1, 0, 3)
    comb = np.zeros((DC, 128, 1, 128), np.float32)
    comb[:, 0:64, 0, :] = g["L0_ffn_B_down"].reshape(64, DC, 128).transpose(1, 0, 2)
    comb[:, 64:72, 0, :] = g["L0_ffn_sB_down"].reshape(R, DC, 128).transpose(1, 0, 2)
    out["d_cols"] = np.ascontiguousarray(
        np.concatenate([a, comb], axis=2).reshape(DC * 128, 44 * 128))

    def pack(Aexp, sA, n, nb):
        p = np.concatenate([Aexp.transpose(1, 0, 2).reshape(n, 64), sA], axis=1)
        return np.ascontiguousarray(
            p.reshape(nb, 128, PK).transpose(1, 0, 2).reshape(128, nb * PK))

    out["gpack"] = pack(g["L0_ffn_A_gate"], g["L0_ffn_sA_gate"], D, DC)
    out["upack"] = pack(g["L0_ffn_A_up"], g["L0_ffn_sA_up"], D, DC)
    out["dpack"] = pack(g["L0_ffn_A_down"], g["L0_ffn_sA_down"], DFF, FC)

    out["wr_t"] = np.ascontiguousarray(
        g["L0_ffn_W_router"].reshape(DC, 128, E).transpose(1, 0, 2).reshape(128, DC * E))

    out["ident"] = np.eye(128, dtype=np.float32)
    onesx = np.zeros((128, THW), np.float32)
    onesx[0, :] = 1.0
    out["onesx"] = onesx
    pat = np.zeros((E, 64), np.float32)
    for e in range(E):
        pat[e, e * 8:(e + 1) * 8] = 1.0
    out["pattern"] = pat
    out["ebias_b"] = np.ascontiguousarray(np.tile(g["L0_ffn_expert_bias"][None, :], (128, 1)))
    cst = np.zeros((128, 8), np.float32)
    cst[:, 0] = 1e-8
    cst[:, 1] = 1e-5
    cst[:, 2] = -1.0
    cst[:, 3] = -0.5
    out["consts"] = cst
    return xs, out


def build_nc():
    nc = bass.Bass("TRN2", target_bir_lowering=False)

    x_h = nc.dram_tensor("x", [T, D], F32, kind="ExternalInput").ap()
    wq_h = nc.dram_tensor("wq_cols", [DC * 128, 17 * 128], F32, kind="ExternalInput").ap()
    wo_h = nc.dram_tensor("wo_cols", [DC * 128, 17 * 128], F32, kind="ExternalInput").ap()
    wkx_h = nc.dram_tensor("wkx", [8 * 128, 17 * 256], F32, kind="ExternalInput").ap()
    wvx_h = nc.dram_tensor("wvx", [8 * 128, 17 * 256], F32, kind="ExternalInput").ap()
    gcols_h = nc.dram_tensor("g_cols", [FC * 128, 17 * 128], F32, kind="ExternalInput").ap()
    ucols_h = nc.dram_tensor("u_cols", [FC * 128, 17 * 128], F32, kind="ExternalInput").ap()
    dcols_h = nc.dram_tensor("d_cols", [DC * 128, 44 * 128], F32, kind="ExternalInput").ap()
    gpack_h = nc.dram_tensor("gpack", [128, DC * PK], F32, kind="ExternalInput").ap()
    upack_h = nc.dram_tensor("upack", [128, DC * PK], F32, kind="ExternalInput").ap()
    dpack_h = nc.dram_tensor("dpack", [128, FC * PK], F32, kind="ExternalInput").ap()
    wr_h = nc.dram_tensor("wr_t", [128, DC * E], F32, kind="ExternalInput").ap()
    id_h = nc.dram_tensor("ident", [128, 128], F32, kind="ExternalInput").ap()
    onesx_h = nc.dram_tensor("onesx", [128, THW], F32, kind="ExternalInput").ap()
    pat_h = nc.dram_tensor("pattern", [E, 64], F32, kind="ExternalInput").ap()
    ebias_h = nc.dram_tensor("ebias_b", [128, E], F32, kind="ExternalInput").ap()
    cst_h = nc.dram_tensor("consts", [128, 8], F32, kind="ExternalInput").ap()
    out_h = nc.dram_tensor("out", [T, D], F32, kind="ExternalOutput").ap()

    dbg = {}
    if DEBUG_DUMP:
        dbg["d_xn"] = nc.dram_tensor("d_xn", [128, DC * T], BF16, kind="ExternalOutput").ap()
        dbg["d_ktm"] = nc.dram_tensor("d_ktm", [128, NT * D], BF16, kind="ExternalOutput").ap()
        dbg["d_vaug"] = nc.dram_tensor("d_vaug", [128, NT * H * 129], BF16, kind="ExternalOutput").ap()
        dbg["d_scores"] = nc.dram_tensor("d_scores", [128, NT * E], F32, kind="ExternalOutput").ap()
        dbg["d_wtm"] = nc.dram_tensor("d_wtm", [128, NT * E], F32, kind="ExternalOutput").ap()
        dbg["d_wb"] = nc.dram_tensor("d_wb", [64, T], F32, kind="ExternalOutput").ap()
        dbg["d_kvp"] = nc.dram_tensor("d_kvp", [128, H * 129], F32, kind="ExternalOutput").ap()
        dbg["d_kvbf"] = nc.dram_tensor("d_kvbf", [128, H * 129], BF16, kind="ExternalOutput").ap()
        dbg["d_qfm"] = nc.dram_tensor("d_qfm", [128, H * T], BF16, kind="ExternalOutput").ap()
        dbg["d_attn"] = nc.dram_tensor("d_attn", [128, H * T], BF16, kind="ExternalOutput").ap()
        dbg["d_ffn"] = nc.dram_tensor("d_ffn", [D, T], F32, kind="ExternalOutput").ap()
        dbg["d_hid"] = nc.dram_tensor("d_hid", [128, FC * THW], BF16, kind="ExternalOutput").ap()
        dbg["d_xallg"] = nc.dram_tensor("d_xallg", [128, THW], BF16, kind="ExternalOutput").ap()
        dbg["d_xallu"] = nc.dram_tensor("d_xallu", [128, THW], BF16, kind="ExternalOutput").ap()
        dbg["d_xalld"] = nc.dram_tensor("d_xalld", [128, THW], BF16, kind="ExternalOutput").ap()
    cc_in = nc.dram_tensor("cc_in", [H * 128, 129], F32)
    cc_out = nc.dram_tensor("cc_out", [H * 128, 129], F32)
    scratch = nc.dram_tensor("ffn_scratch", [D, T], F32)
    dump3 = nc.dram_tensor("dump3", [T, D], F32)

    st = ExitStack()
    P = {}
    with st:
        def sb(name, shape, dt=F32, stk=None, side=None):
            return (stk or st).enter_context(nc.sbuf_tensor(name, shape, dt, side=side))

        pe = Ctr(st.enter_context(nc.semaphore("pe")))
        dv = Ctr(st.enter_context(nc.semaphore("dv")))
        ac = Ctr(st.enter_context(nc.semaphore("ac")))
        dg = Ctr(st.enter_context(nc.semaphore("dg")))
        cc = Ctr(st.enter_context(nc.semaphore("cc")))
        xld = [Ctr(st.enter_context(nc.semaphore(f"xld{i}"))) for i in range(2)]
        kvs = [Ctr(st.enter_context(nc.semaphore(f"kvs{i}"))) for i in range(2)]
        qcs = [Ctr(st.enter_context(nc.semaphore(f"qcs{i}"))) for i in range(2)]
        kk = Ctr(st.enter_context(nc.semaphore("kk")))
        pc = Ctr(st.enter_context(nc.semaphore("pc")))
        osts = [Ctr(st.enter_context(nc.semaphore(f"ost{i}"))) for i in range(2)]
        ds = xld[0]
        do = cc
        kst = kk
        kbf = kk
        gus = [Ctr(st.enter_context(nc.semaphore(f"gus{i}"))) for i in range(2)]
        dcs = [Ctr(st.enter_context(nc.semaphore(f"dcs{i}"))) for i in range(2)]
        wos = [Ctr(st.enter_context(nc.semaphore(f"wos{i}"))) for i in range(2)]
        fws = [Ctr(st.enter_context(nc.semaphore(f"fws{i}"))) for i in range(2)]
        frs = [Ctr(st.enter_context(nc.semaphore(f"frs{i}"))) for i in range(3)]

        blk = st.enter_context(nc.Block())

        def vpub(v, k=1):
            v.drain().then_inc(dv.sem, k)
            return dv.add(k)

        def apub(sa, k=1):
            sa.drain().then_inc(ac.sem, k)
            return ac.add(k)

        id_bf = sb("id_bf", [128, 128], BF16)
        id_f32 = sb("id_f32", [128, 128], F32)
        onesx_bf = sb("onesx_bf", [128, THW], BF16)
        pat_sb = sb("pat_sb", [E, 64], F32)
        ebias_sb = sb("ebias_sb", [128, E], F32)
        cst_sb = sb("cst_sb", [128, 8], F32)
        gpack_sb = sb("gpack_sb", [128, DC * PK], BF16)
        upack_sb = sb("upack_sb", [128, DC * PK], BF16)
        dpack_sb = sb("dpack_sb", [128, FC * PK], BF16)
        wr_sb = sb("wr_sb", [128, DC * E], BF16)
        scores = sb("scores", [128, NT * E], F32)
        w_tm = sb("w_tm", [128, NT * E], F32)
        w_fm = sb("w_fm", [E, T], F32)
        w_bcast = sb("w_bcast", [64, T], F32)
        kv_part = sb("kv_part", [128, H * 129], F32)
        kv_bf = sb("kv_bf", [128, H * 129], BF16)
        xallg = sb("xallg", [128, THW], BF16)
        xallu = sb("xallu", [128, THW], BF16)
        xalld = sb("xalld", [128, THW], BF16)
        xa_tmp = sb("xa_tmp", [128, THW], F32)
        tk = sb("tk", [128, 16], F32)

        pool = [st.enter_context(nc.psum_tensor(f"ps{i}", [128, 512], F32)) for i in range(8)]

        C_EPS8 = cst_sb[:, 0:1]
        C_EPSLN = cst_sb[:, 1:2]
        C_NEG1 = cst_sb[:, 2:3]
        C_NEGH = cst_sb[:, 3:4]

        @blk.gpsimd
        def _(gp):
            for dst, src in ((id_bf, id_h), (onesx_bf, onesx_h), (wr_sb, wr_h),
                             (gpack_sb, gpack_h), (upack_sb, upack_h), (dpack_sb, dpack_h)):
                gp.dma_start(out=dst[:, :], in_=src).then_inc(dg.sem, 16)
                dg.add(16)
        consts_g = dg.n

        @blk.sync
        def _(s):
            for dst, src in ((id_f32, id_h), (pat_sb, pat_h), (ebias_sb, ebias_h),
                             (cst_sb, cst_h)):
                s.dma_start(out=dst[:, :], in_=src).then_inc(ds.sem, 16)
                ds.add(16)
        consts_s = ds.n

        @blk.vector
        def _(v):
            v.memset(xallg[:, :], 0.0)
            v.memset(xallu[:, :], 0.0)
            v.memset(xalld[:, :], 0.0)
            vpub(v)

        stXN = ExitStack()
        xn_fm = sb("xn_fm", [128, DC * T], BF16, stXN)
        stAB = ExitStack()
        x2 = [sb(f"x2_{i}", [128, D], F32, stAB) for i in range(2)]
        xn_tm = [sb(f"xn_tm{i}", [128, D], BF16, stAB) for i in range(2)]
        stats = sb("stats", [128, 10], F32, stAB)
        k_tm = sb("k_tm", [128, NT * D], BF16, stAB)
        v_aug = sb("v_aug", [128, NT * H * 129], BF16, stAB)
        kvm = {m: [sb(f"kvm_{m}{i}", [128, 17 * 256], BF16, stAB) for i in range(2)]
               for m in ("k", "v")}
        elu_t = [sb(f"elu_t{i}", [128, 512], F32, stAB) for i in range(2)]
        qcol = [sb(f"qcol{i}", [128, 17 * 128], BF16, side="right") for i in range(2)]
        wq_v = wq_h.rearrange("(oc p) m -> oc p m", p=128)

        def emit_qcol_load(oc):
            @blk.gpsimd
            def _(gp, oc=oc):
                if oc < 2:
                    gp.wait_ge(pe.sem, P["a_end_pe"])
                    gp.wait_ge(dv.sem, xn_all)
                    gp.wait_ge(ac.sem, P["a_end_ac"])
                    gp.wait_ge(pc.sem, 128)
                else:
                    gp.wait_ge(pe.sem, P[("qmm", oc - 2)])
                gp.dma_start(out=qcol[oc % 2][:, :], in_=wq_v[oc]).then_inc(qcs[oc % 2].sem, 16)
                P[("qdma", oc)] = qcs[oc % 2].add(16)

        x_v = x_h.rearrange("(tc p) d -> tc p d", p=128)
        xn_view = xn_fm[:, :].rearrange("p (dc t) -> p dc t", t=T)

        for tc in range(NT):
            @blk.sync
            def _(s, tc=tc):
                if tc >= 2:
                    s.wait_ge(dv.sem, P[("xfree", tc - 2)])
                s.dma_start(out=x2[tc % 2][:, :], in_=x_v[tc]).then_inc(xld[tc % 2].sem, 16)
                P[("xdma", tc)] = xld[tc % 2].add(16)

            @blk.vector
            def _(v, tc=tc):
                v.wait_ge(xld[tc % 2].sem, P[("xdma", tc)])
                v.reduce_sum(stats[:, 0:1], x2[tc % 2][:, :], AX.X)
                P[("sum", tc)] = vpub(v)

            @blk.scalar
            def _(sa, tc=tc):
                sa.wait_ge(xld[tc % 2].sem, P[("xdma", tc)])
                sa.wait_ge(ds.sem, consts_s)
                sa.activation(xn_tm[tc % 2][:, :], x2[tc % 2][:, :], AF.Square,
                              accum_out=stats[:, 1:2])
                P[("sq", tc)] = apub(sa)

            @blk.vector
            def _(v, tc=tc):
                v.wait_ge(ac.sem, P[("sq", tc)])
                v.wait_ge(dv.sem, P[("sum", tc)])
                v.tensor_scalar(stats[:, 2:3], stats[:, 0:1], 1.0 / D, None, ALU.mult)
                v.tensor_scalar(stats[:, 3:4], stats[:, 1:2], 1.0 / D, None, ALU.mult)
                n = vpub(v)
                v.wait_ge(dv.sem, n)
                v.tensor_mul(stats[:, 4:5], stats[:, 2:3], stats[:, 2:3])
                n = vpub(v)
                v.wait_ge(dv.sem, n)
                v.tensor_sub(stats[:, 5:6], stats[:, 3:4], stats[:, 4:5])
                P[("var", tc)] = vpub(v)

            @blk.scalar
            def _(sa, tc=tc):
                sa.wait_ge(dv.sem, P[("var", tc)])
                sa.activation(stats[:, 6:7], stats[:, 5:6], AF.Ln, bias=C_EPSLN)
                n = apub(sa)
                sa.wait_ge(ac.sem, n)
                sa.activation(stats[:, 7:8], stats[:, 6:7], AF.Exp, scale=C_NEGH)
                P[("rstd", tc)] = apub(sa)

            @blk.vector
            def _(v, tc=tc):
                v.wait_ge(ac.sem, P[("rstd", tc)])
                v.tensor_scalar(xn_tm[tc % 2][:, :], x2[tc % 2][:, :],
                                stats[:, 2:3], stats[:, 7:8], ALU.subtract, ALU.mult)
                P[("xn", tc)] = P[("xfree", tc)] = vpub(v)

            @blk.sync
            def _(s, tc=tc):
                s.wait_ge(xld[tc % 2].sem, P[("xdma", tc)])
                s.dma_start(out=out_h[tc * 128:(tc + 1) * 128, :],
                            in_=x2[tc % 2][:, :]).then_inc(pc.sem, 16)
                P[("pcst", tc)] = pc.add(16)

            for gq in range(4):
                bank = 6 + (gq % 2)

                @blk.tensor
                def _(t, tc=tc, gq=gq, bank=bank):
                    t.wait_ge(dv.sem, P[("xn", tc)])
                    t.wait_ge(dg.sem, consts_g)
                    key = ("trfree", bank)
                    if key in P:
                        t.wait_ge(dv.sem, P[key])
                    for q in range(4):
                        inst = t.matmul(pool[bank][:, q * 128:(q + 1) * 128],
                                        xn_tm[tc % 2][:, (4 * gq + q) * 128:(4 * gq + q + 1) * 128],
                                        id_bf[:, :], start=True, stop=True)
                    inst.then_inc(pe.sem, 1)
                    P[("trmm", tc, gq)] = pe.add(1)

                @blk.vector
                def _(v, tc=tc, gq=gq, bank=bank):
                    v.wait_ge(pe.sem, P[("trmm", tc, gq)])
                    dst = xn_view[:, 4 * gq:4 * gq + 4, tc * 128:(tc + 1) * 128]
                    src = pool[bank][:, :].rearrange("p (q n) -> p q n", n=128)
                    v.tensor_copy(dst, src)
                    P[("trfree", bank)] = vpub(v)
            P[("xnready", tc)] = P[("trfree", 6 + (3 % 2))]
        xn_all = dv.n
        P["a_end_pe"] = pe.n
        P["a_end_ac"] = ac.n
        P["a_end_pe"] = pe.n
        P["a_end_ac"] = ac.n

        wkx_v = wkx_h.rearrange("(oq p) m -> oq p m", p=128)
        wvx_v = wvx_h.rearrange("(oq p) m -> oq p m", p=128)
        NQ = 8

        for oq in range(NQ):
            if oq == 6:
                emit_qcol_load(0)
                emit_qcol_load(1)

            @blk.gpsimd
            def _(gp, oq=oq):
                if oq >= 2:
                    gp.wait_ge(pe.sem, P[("kvbuf", oq - 2)])
                for mv, srcv in (("k", wkx_v), ("v", wvx_v)):
                    gp.dma_start(out=kvm[mv][oq % 2][:, :],
                                 in_=srcv[oq]).then_inc(kvs[oq % 2].sem, 16)
                    kvs[oq % 2].add(16)
                P[("kvdma", oq)] = kvs[oq % 2].n

            for tc in range(NT):
                @blk.tensor
                def _(t, oq=oq, tc=tc):
                    t.wait_ge(kvs[oq % 2].sem, P[("kvdma", oq)])
                    if oq == 0:
                        t.wait_ge(dv.sem, P[("xnready", tc)])
                    for key in (("kbank", tc % 2), ("vbank", tc % 2)):
                        if key in P:
                            t.wait_ge(dv.sem, P[key])
                    if oq == 0 and ("rbank", tc % 2) in P:
                        t.wait_ge(ac.sem, P[("rbank", tc % 2)])
                    for c in range(17):
                        stat = (xn_fm[:, c * T + tc * 128:c * T + (tc + 1) * 128]
                                if c < 16 else onesx_bf[:, 0:128])
                        t.matmul(pool[tc % 2][:, 0:256], stat,
                                 kvm["k"][oq % 2][:, c * 256:(c + 1) * 256],
                                 start=(c == 0), stop=(c == 16), skip_group_check=True)
                        iV = t.matmul(pool[2 + tc % 2][:, 0:256], stat,
                                      kvm["v"][oq % 2][:, c * 256:(c + 1) * 256],
                                      start=(c == 0), stop=(c == 16), skip_group_check=True)
                        if oq == 0 and c < 16:
                            iR = t.matmul(pool[4 + tc % 2][:, 0:E], stat,
                                          wr_sb[:, c * E:(c + 1) * E],
                                          start=(c == 0), stop=(c == 15),
                                          skip_group_check=True)
                            if c == 15:
                                iR.then_inc(pe.sem, 1)
                                P[("rmm", tc)] = pe.add(1)
                    iV.then_inc(pe.sem, 1)
                    P[("kvmm", oq, tc)] = pe.add(1)
                    if tc == NT - 1:
                        P[("kvbuf", oq)] = pe.n

                @blk.scalar
                def _(sa, oq=oq, tc=tc):
                    sa.wait_ge(pe.sem, P[("kvmm", oq, tc)])
                    key = ("kbank", tc % 2)
                    if key in P:
                        sa.wait_ge(dv.sem, P[key])
                    sa.activation(elu_t[tc % 2][:, 0:256], pool[tc % 2][:, 0:256], AF.Exp)
                    P[("kexp", oq, tc)] = apub(sa)

                @blk.vector
                def _(v, oq=oq, tc=tc):
                    v.wait_ge(ac.sem, P[("kexp", oq, tc)])
                    v.tensor_scalar(elu_t[tc % 2][:, 256:512], elu_t[tc % 2][:, 0:256],
                                    1.0, None, ALU.min)
                    n = vpub(v)
                    v.wait_ge(dv.sem, n)
                    v.scalar_tensor_tensor(
                        k_tm[:, tc * D + oq * 256: tc * D + (oq + 1) * 256],
                        pool[tc % 2][:, 0:256], 0.0, elu_t[tc % 2][:, 256:512],
                        ALU.max, ALU.add)
                    P[("kbank", tc % 2)] = vpub(v)
                    va = v_aug[:, :].rearrange("p (tc h e) -> p tc h e", h=H, e=129)
                    v.tensor_copy(va[:, tc, 2 * oq:2 * oq + 2, 0:128],
                                  pool[2 + tc % 2][:, 0:256].rearrange(
                                      "p (h e) -> p h e", e=128))
                    P[("vbank", tc % 2)] = vpub(v)

                if oq == 0:
                    @blk.scalar
                    def _(sa, tc=tc):
                        sa.wait_ge(pe.sem, P[("rmm", tc)])
                        sa.activation(scores[:, tc * E:(tc + 1) * E],
                                      pool[4 + tc % 2][:, 0:E], AF.Sigmoid)
                        P[("sig", tc)] = P[("rbank", tc % 2)] = apub(sa)

        @blk.vector
        def _(v):
            va = v_aug[:, :].rearrange("p (tc h e) -> p tc h e", h=H, e=129)
            v.memset(va[:, :, :, 128:129], 1.0)
            P["vones"] = vpub(v)

        for h in range(H):
            @blk.tensor
            def _(t, h=h):
                t.wait_ge(dv.sem, P["vones"])
                if h >= 2:
                    t.wait_ge(dv.sem, P[("kvevac", h - 2)])
                else:
                    t.wait_ge(ac.sem, P[("sig", NT - 1)])
                t.wait_ge(dv.sem, max(P[("kbank", 0)], P[("kbank", 1)],
                                      P[("vbank", 0)], P[("vbank", 1)]))
                for tc in range(NT):
                    inst = t.matmul(
                        pool[4 + h % 2][:, 0:129],
                        k_tm[:, tc * D + h * 128: tc * D + (h + 1) * 128],
                        v_aug[:, tc * H * 129 + h * 129: tc * H * 129 + (h + 1) * 129],
                        start=(tc == 0), stop=(tc == NT - 1))
                inst.then_inc(pe.sem, 1)
                P[("kvh", h)] = pe.add(1)

            @blk.vector
            def _(v, h=h):
                v.wait_ge(pe.sem, P[("kvh", h)])
                v.tensor_copy(kv_part[:, h * 129:(h + 1) * 129], pool[4 + h % 2][:, 0:129])
                P[("kvevac", h)] = vpub(v)

        @blk.sync
        def _(s):
            s.wait_ge(dv.sem, P[("kvevac", H - 1)])
            dst = cc_in.ap().rearrange("(h d) e -> d h e", d=128)
            src = kv_part[:, :].rearrange("p (h e) -> p h e", e=129)
            s.dma_start(out=dst[:, 0:8, :], in_=src[:, 0:8, :]).then_inc(kst.sem, 16)
            s.dma_start(out=dst[:, 8:16, :], in_=src[:, 8:16, :]).then_inc(kst.sem, 16)
            P["kvstore"] = kst.add(32)

        if DEBUG_DUMP:
            @blk.sync
            def _(s):
                s.wait_ge(dv.sem, dv.n)
                s.wait_ge(ac.sem, ac.n)
                s.dma_start(out=dbg["d_xn"], in_=xn_fm[:, :]).then_inc(do.sem, 16)
                s.dma_start(out=dbg["d_ktm"], in_=k_tm[:, :]).then_inc(do.sem, 16)
                s.dma_start(out=dbg["d_vaug"], in_=v_aug[:, :]).then_inc(do.sem, 16)
                s.dma_start(out=dbg["d_kvp"], in_=kv_part[:, :]).then_inc(do.sem, 16)
                s.dma_start(out=dbg["d_scores"], in_=scores[:, :]).then_inc(do.sem, 16)
                do.add(5 * 16)
                s.wait_ge(do.sem, do.n)
            P["dbg1"] = do.n
        b_end_pe = pe.n
        b_end_dv = dv.n
        stAB.close()

        for tc in range(NT):
            @blk.vector
            def _(v, tc=tc):
                sc = scores[:, tc * E:(tc + 1) * E]
                v.wait_ge(ac.sem, P[("sig", tc)])
                v.tensor_add(tk[:, 0:8], sc, ebias_sb[:, :])
                n = vpub(v)
                v.wait_ge(dv.sem, n)
                v.reduce_max(tk[:, 8:9], tk[:, 0:8], AX.X)
                n = vpub(v)
                v.wait_ge(dv.sem, n)
                v.tensor_scalar(tk[:, 0:8], tk[:, 0:8], tk[:, 8:9], None, ALU.is_equal)
                P[("eq", tc)] = vpub(v)

            @blk.vector
            def _(v, tc=tc):
                sc = scores[:, tc * E:(tc + 1) * E]
                v.wait_ge(dv.sem, P[("eq", tc)])
                v.tensor_add(xa_tmp[:, 0:8], sc, ebias_sb[:, :])
                n = vpub(v)
                v.wait_ge(dv.sem, n)
                v.scalar_tensor_tensor(xa_tmp[:, 8:16], tk[:, 0:8], -1e30,
                                       xa_tmp[:, 0:8], ALU.mult, ALU.add)
                n = vpub(v)
                v.wait_ge(dv.sem, n)
                v.reduce_max(tk[:, 8:9], xa_tmp[:, 8:16], AX.X)
                n = vpub(v)
                v.wait_ge(dv.sem, n)
                v.tensor_scalar(xa_tmp[:, 0:8], xa_tmp[:, 0:8], tk[:, 8:9], None,
                                ALU.is_ge)
                n = vpub(v)
                v.wait_ge(dv.sem, n)
                v.tensor_mul(w_tm[:, tc * E:(tc + 1) * E], sc, xa_tmp[:, 0:8])
                n = vpub(v)
                v.wait_ge(dv.sem, n)
                v.reduce_sum(tk[:, 8:9], w_tm[:, tc * E:(tc + 1) * E], AX.X)
                P[("den", tc)] = vpub(v)

            @blk.scalar
            def _(sa, tc=tc):
                sa.wait_ge(dv.sem, P[("den", tc)])
                sa.activation(tk[:, 10:11], tk[:, 8:9], AF.Ln, bias=C_EPS8)
                n = apub(sa)
                sa.wait_ge(ac.sem, n)
                sa.activation(tk[:, 11:12], tk[:, 10:11], AF.Exp, scale=C_NEG1)
                P[("denr", tc)] = apub(sa)

            @blk.vector
            def _(v, tc=tc):
                v.wait_ge(ac.sem, P[("denr", tc)])
                v.tensor_scalar(w_tm[:, tc * E:(tc + 1) * E], w_tm[:, tc * E:(tc + 1) * E],
                                tk[:, 11:12], None, ALU.mult)
                P[("wtm", tc)] = vpub(v)

        stQ = ExitStack()
        q_fm = sb("q_fm", [128, H * T], BF16, stQ, side="right")
        stC = ExitStack()
        qt = [sb("qt0", [128, 512], F32, stC)] * 2

        for oc in range(H):
            if oc >= 2:
                emit_qcol_load(oc)

            @blk.tensor
            def _(t, oc=oc):
                t.wait_ge(qcs[oc % 2].sem, P[("qdma", oc)])
                for b in range(2):
                    key = ("qbank", (oc % 2) * 2 + b)
                    if key in P:
                        t.wait_ge(dv.sem, P[key])
                    elif oc < 2:
                        t.wait_ge(dv.sem, b_end_dv)
                for c in range(17):
                    for th in range(TH):
                        bank = (oc % 2) * 2 + th
                        mov = (xn_fm[:, c * T + th * THW: c * T + (th + 1) * THW]
                               if c < 16 else onesx_bf[:, :])
                        inst = t.matmul(pool[bank][:, :],
                                        qcol[oc % 2][:, c * 128:(c + 1) * 128],
                                        mov, start=(c == 0), stop=(c == 16),
                                        skip_group_check=True)
                inst.then_inc(pe.sem, 1)
                P[("qmm", oc)] = pe.add(1)

            for th in range(TH):
                bank = (oc % 2) * 2 + th

                @blk.scalar
                def _(sa, oc=oc, th=th, bank=bank):
                    sa.wait_ge(pe.sem, P[("qmm", oc)])
                    key = "qtmp"
                    if key in P:
                        sa.wait_ge(dv.sem, P[key])
                    sa.activation(qt[th][:, :], pool[bank][:, :], AF.Exp)
                    P[("qexp", oc, th)] = apub(sa)

                @blk.vector
                def _(v, oc=oc, th=th, bank=bank):
                    v.wait_ge(ac.sem, P[("qexp", oc, th)])
                    v.tensor_scalar(qt[th][:, :], qt[th][:, :], 1.0, None, ALU.min)
                    n = vpub(v)
                    v.wait_ge(dv.sem, n)
                    v.scalar_tensor_tensor(
                        q_fm[:, oc * T + th * THW: oc * T + (th + 1) * THW],
                        pool[bank][:, :], 0.0, qt[th][:, :], ALU.max, ALU.add)
                    P[("qbank", bank)] = P["qtmp"] = vpub(v)
        for tc in range(NT):
            @blk.tensor
            def _(t, tc=tc):
                t.wait_ge(dv.sem, P[("wtm", tc)])
                t.wait_ge(ds.sem, consts_s)
                if tc >= 2:
                    t.wait_ge(dv.sem, P[("wfm", tc - 2)])
                else:
                    t.wait_ge(dv.sem, max(P[("qbank", 2)], P[("qbank", 3)]))
                inst = t.matmul(pool[2 + tc % 2][0:E, 0:128],
                                w_tm[:, tc * E:(tc + 1) * E], id_f32[:, :],
                                start=True, stop=True)
                inst.then_inc(pe.sem, 1)
                P[("wtr", tc)] = pe.add(1)

            @blk.vector
            def _(v, tc=tc):
                v.wait_ge(pe.sem, P[("wtr", tc)])
                v.tensor_copy(w_fm[:, tc * 128:(tc + 1) * 128], pool[2 + tc % 2][0:E, 0:128])
                P[("wfm", tc)] = vpub(v)

        for half in range(2):
            @blk.tensor
            def _(t, half=half):
                t.wait_ge(dv.sem, P[("wfm", NT - 1)])
                inst = t.matmul(pool[2 + half][0:64, 0:512], pat_sb[:, :],
                                w_fm[:, half * 512:(half + 1) * 512],
                                start=True, stop=True)
                inst.then_inc(pe.sem, 1)
                P[("wbc_mm", half)] = pe.add(1)

            @blk.vector
            def _(v, half=half):
                v.wait_ge(pe.sem, P[("wbc_mm", half)])
                v.tensor_copy(w_bcast[:, half * 512:(half + 1) * 512],
                              pool[2 + half][0:64, 0:512])
                P[("wbc", half)] = vpub(v)

        P["qdone_pe"] = pe.n
        c_end_dv = dv.n

        @blk.gpsimd
        def _(gp):
            gp.wait_ge(kst.sem, P["kvstore"])
            gp.collective_compute(
                "AllReduce", ALU.add,
                replica_groups=[[0, 1, 2, 3], [4, 5, 6, 7]],
                ins=[cc_in.ap().opt()], outs=[cc_out.ap().opt()],
            ).then_inc(cc.sem)
            cc.add(1)

        def emit_kv_readback():
            @blk.gpsimd
            def _(gp):
                gp.wait_ge(cc.sem, 1)
                dst = kv_bf[:, :].rearrange("p (h e) -> p h e", e=129)
                srcv = cc_out.ap().rearrange("(h d) e -> d h e", d=128)
                gp.dma_start(out=dst[:, 0:8, :], in_=srcv[:, 0:8, :]).then_inc(kbf.sem, 16)
                gp.dma_start(out=dst[:, 8:16, :], in_=srcv[:, 8:16, :]).then_inc(kbf.sem, 16)
                P["kvbf"] = kbf.add(32)

        if DEBUG_DUMP:
            @blk.sync
            def _(s):
                s.wait_ge(dv.sem, dv.n)
                s.wait_ge(kbf.sem, P["kvbf"])
                s.dma_start(out=dbg["d_qfm"], in_=q_fm[:, :]).then_inc(do.sem, 16)
                s.dma_start(out=dbg["d_wtm"], in_=w_tm[:, :]).then_inc(do.sem, 16)
                s.dma_start(out=dbg["d_wb"], in_=w_bcast[:, :]).then_inc(do.sem, 16)
                s.dma_start(out=dbg["d_kvbf"], in_=kv_bf[:, :]).then_inc(do.sem, 16)
                do.add(4 * 16)
                s.wait_ge(do.sem, do.n)

        stE = ExitStack()
        gcol = [sb(f"gcol{i}", [128, 17 * 128], BF16, stE) for i in range(2)]
        ucol = [sb(f"ucol{i}", [128, 17 * 128], BF16, stE) for i in range(2)]
        dcol = [sb(f"dcol{i}", [128, 44 * 128], BF16, stE) for i in range(2)]
        hid = sb("hid", [128, FC * THW], BF16, stE)
        silu_t = [sb(f"silu{i}", [128, THW], F32, stE) for i in range(2)]
        fstage = [sb(f"fstage{i}", [128, THW], F32, stE) for i in range(2)]
        gcols_v = gcols_h.rearrange("(fc p) m -> fc p m", p=128)
        ucols_v = ucols_h.rearrange("(fc p) m -> fc p m", p=128)
        dcols_v = dcols_h.rearrange("(dc p) m -> dc p m", p=128)

        if not SKIP_FFN:
            for th in range(TH):
                for nm, pk_sb, bank in (("g", gpack_sb, 4), ("u", upack_sb, 5)):
                    @blk.tensor
                    def _(t, th=th, nm=nm, pk_sb=pk_sb, bank=bank):
                        if th == 0 and nm == "g":
                            t.wait_ge(pe.sem, P["qdone_pe"])
                            t.wait_ge(dv.sem, c_end_dv)
                        if th == 1:
                            t.wait_ge(dv.sem, P[("xa_ev", nm, 0)])
                        for c in range(DC):
                            inst = t.matmul(pool[bank][0:PK, 0:THW],
                                            pk_sb[:, c * PK:(c + 1) * PK],
                                            xn_fm[:, c * T + th * THW: c * T + (th + 1) * THW],
                                            start=(c == 0), stop=(c == DC - 1))
                        inst.then_inc(pe.sem, 1)
                        P[("xamm", nm, th)] = pe.add(1)

                    @blk.vector
                    def _(v, th=th, nm=nm, bank=bank):
                        xall = xallg if nm == "g" else xallu
                        v.wait_ge(pe.sem, P[("xamm", nm, th)])
                        v.wait_ge(dv.sem, max(P[("wbc", 0)], P[("wbc", 1)]))
                        if th == 1:
                            v.wait_ge(pe.sem, P[("gumm", 42, th - 1)])
                        v.tensor_scalar(xa_tmp[0:64, 0:THW], pool[bank][0:64, 0:THW],
                                        0.125, None, ALU.mult)
                        n = vpub(v)
                        v.wait_ge(dv.sem, n)
                        v.tensor_mul(xall[0:64, :], xa_tmp[0:64, 0:THW],
                                     w_bcast[0:64, th * THW:(th + 1) * THW])
                        n = vpub(v)
                        v.tensor_scalar(xall[64:PK, :], pool[bank][64:PK, 0:THW],
                                        0.125, None, ALU.mult)
                        P[("xa_ev", nm, th)] = vpub(v)

                for fc in range(FC):
                    @blk.gpsimd
                    def _(gp, th=th, fc=fc):
                        idx = th * FC + fc
                        if idx >= 2:
                            pth, pfc = divmod(idx - 2, FC)
                            gp.wait_ge(pe.sem, P[("gumm", pfc, pth)])
                        if idx < 2:
                            gp.wait_ge(pe.sem, b_end_pe)
                        gp.dma_start(out=gcol[fc % 2][:, :], in_=gcols_v[fc]).then_inc(gus[fc % 2].sem, 16)
                        gp.dma_start(out=ucol[fc % 2][:, :], in_=ucols_v[fc]).then_inc(gus[fc % 2].sem, 16)
                        P[("gudma", fc, th)] = gus[fc % 2].add(32)

                    @blk.tensor
                    def _(t, th=th, fc=fc):
                        t.wait_ge(gus[fc % 2].sem, P[("gudma", fc, th)])
                        t.wait_ge(dv.sem, P[("xa_ev", "g", th)])
                        t.wait_ge(dv.sem, P[("xa_ev", "u", th)])
                        kk = ("gubank", fc % 2)
                        if kk in P:
                            t.wait_ge(dv.sem, P[kk])
                        for c in range(17):
                            movg = (xn_fm[:, c * T + th * THW: c * T + (th + 1) * THW]
                                    if c < 16 else xallg[:, :])
                            movu = (xn_fm[:, c * T + th * THW: c * T + (th + 1) * THW]
                                    if c < 16 else xallu[:, :])
                            t.matmul(pool[fc % 2][:, :], gcol[fc % 2][:, c * 128:(c + 1) * 128],
                                     movg, start=(c == 0), stop=(c == 16), skip_group_check=True)
                            iU = t.matmul(pool[2 + fc % 2][:, :],
                                          ucol[fc % 2][:, c * 128:(c + 1) * 128],
                                          movu, start=(c == 0), stop=(c == 16),
                                          skip_group_check=True)
                        if fc >= 2:
                            j = fc - 2
                            t.matmul(pool[4][0:PK, 0:THW],
                                     dpack_sb[:, j * PK:(j + 1) * PK],
                                     hid[:, j * THW:(j + 1) * THW],
                                     start=(j == 0), stop=False, skip_group_check=True)
                        iU.then_inc(pe.sem, 1)
                        P[("gumm", fc, th)] = pe.add(1)

                    @blk.scalar
                    def _(sa, th=th, fc=fc):
                        sa.wait_ge(pe.sem, P[("gumm", fc, th)])
                        key = ("gubank", fc % 2)
                        if key in P:
                            sa.wait_ge(dv.sem, P[key])
                        sa.activation(silu_t[fc % 2][:, :], pool[fc % 2][:, :], AF.Silu)
                        P[("silu", fc, th)] = apub(sa)

                    @blk.vector
                    def _(v, th=th, fc=fc):
                        v.wait_ge(ac.sem, P[("silu", fc, th)])
                        v.tensor_mul(hid[:, fc * THW:(fc + 1) * THW],
                                     silu_t[fc % 2][:, :], pool[2 + fc % 2][:, :])
                        P[("gubank", fc % 2)] = P[("hidp", fc, th)] = vpub(v)

                @blk.tensor
                def _(t, th=th):
                    for j in (FC - 2, FC - 1):
                        t.wait_ge(dv.sem, P[("hidp", j, th)])
                        inst = t.matmul(pool[4][0:PK, 0:THW],
                                        dpack_sb[:, j * PK:(j + 1) * PK],
                                        hid[:, j * THW:(j + 1) * THW],
                                        start=False, stop=(j == FC - 1),
                                        skip_group_check=True)
                    inst.then_inc(pe.sem, 1)
                    P[("xadmm", th)] = pe.add(1)

                @blk.vector
                def _(v, th=th):
                    v.wait_ge(pe.sem, P[("xadmm", th)])
                    v.tensor_scalar(xa_tmp[0:64, 0:THW], pool[4][0:64, 0:THW],
                                    0.125, None, ALU.mult)
                    n = vpub(v)
                    v.wait_ge(dv.sem, n)
                    v.tensor_mul(xalld[0:64, :], xa_tmp[0:64, 0:THW],
                                 w_bcast[0:64, th * THW:(th + 1) * THW])
                    n = vpub(v)
                    v.tensor_scalar(xalld[64:PK, :], pool[4][64:PK, 0:THW],
                                    0.125, None, ALU.mult)
                    P[("xad_ev", th)] = vpub(v)

                if DEBUG_DUMP and th == 0:
                    @blk.sync
                    def _(s):
                        s.wait_ge(dv.sem, P[("xad_ev", 0)])
                        s.dma_start(out=dbg["d_hid"], in_=hid[:, :]).then_inc(do.sem, 16)
                        s.dma_start(out=dbg["d_xallg"], in_=xallg[:, :]).then_inc(do.sem, 16)
                        s.dma_start(out=dbg["d_xallu"], in_=xallu[:, :]).then_inc(do.sem, 16)
                        s.dma_start(out=dbg["d_xalld"], in_=xalld[:, :]).then_inc(do.sem, 16)
                        do.add(4 * 16)
                        s.wait_ge(do.sem, do.n)

                for dc in range(DC):
                    @blk.gpsimd
                    def _(gp, th=th, dc=dc):
                        idx = th * DC + dc
                        if idx >= 2:
                            pth, pdc = divmod(idx - 2, DC)
                            gp.wait_ge(pe.sem, P[("dmm", pdc, pth)])
                        if idx == 0:
                            gp.wait_ge(pe.sem, P[("gumm", FC - 1, 0)])
                        gp.dma_start(out=dcol[dc % 2][:, :], in_=dcols_v[dc]).then_inc(dcs[dc % 2].sem, 16)
                        P[("ddma", dc, th)] = dcs[dc % 2].add(16)

                    @blk.tensor
                    def _(t, th=th, dc=dc):
                        t.wait_ge(dcs[dc % 2].sem, P[("ddma", dc, th)])
                        t.wait_ge(dv.sem, P[("xad_ev", th)])
                        key = ("dbank", dc % 2)
                        if key in P:
                            t.wait_ge(dv.sem, P[key])
                        for c in range(44):
                            mov = (hid[:, c * THW:(c + 1) * THW] if c < FC else xalld[:, :])
                            inst = t.matmul(pool[dc % 2][:, :],
                                            dcol[dc % 2][:, c * 128:(c + 1) * 128],
                                            mov, start=(c == 0), stop=(c == 43))
                        inst.then_inc(pe.sem, 1)
                        P[("dmm", dc, th)] = pe.add(1)

                    @blk.vector
                    def _(v, th=th, dc=dc):
                        v.wait_ge(pe.sem, P[("dmm", dc, th)])
                        if ("fst", dc % 2) in P:
                            v.wait_ge(fws[dc % 2].sem, P[("fst", dc % 2)])
                        v.tensor_copy(fstage[dc % 2][:, :], pool[dc % 2][:, :])
                        P[("dbank", dc % 2)] = P[("fev", dc, th)] = vpub(v)

                    @blk.sync
                    def _(s, th=th, dc=dc):
                        s.wait_ge(dv.sem, P[("fev", dc, th)])
                        s.dma_start(out=scratch.ap()[dc * 128:(dc + 1) * 128,
                                                     th * THW:(th + 1) * THW],
                                    in_=fstage[dc % 2][:, :]).then_inc(fws[dc % 2].sem, 16)
                        P[("fst", dc % 2)] = P[("fsc", dc, th)] = fws[dc % 2].add(16)
                if th == 0:
                    emit_kv_readback()
        e_end_pe = pe.n
        e_end_dv = dv.n
        stE.close()
        stC.close()
        stXN.close()

        stD = ExitStack()
        attn_fm = sb("attn_fm", [128, H * T], BF16, stD)
        t1row = [sb(f"t1row{i}", [1, THW], F32, stD) for i in range(2)]
        rrow = [sb(f"rrow{i}", [1, THW], BF16, stD) for i in range(2)]
        bc_sb = [sb(f"bc{i}", [128, THW], F32, stD) for i in range(2)]
        wocol = [sb(f"wocol{i}", [128, 17 * 128], BF16, stD) for i in range(2)]
        wocol32 = [sb(f"wocol32_{i}", [128, 17 * 128], F32, stD) for i in range(2)]
        s1 = [sb(f"s1_{i}", [128, THW], F32, stD) for i in range(2)]
        fbuf = [sb(f"fbuf{i}", [128, THW], F32, stD) for i in range(3)]
        ostage = sb("ostage", [128, 4 * D], F32, stD)
        wo_v = wo_h.rearrange("(dc p) m -> dc p m", p=128)
        out_q = out_h.rearrange("(q p) d -> p q d", p=128)

        def emit_bc(itb):
            thb, hb = divmod(itb, H)

            @blk.tensor
            def _(t, itb=itb):
                t.wait_ge(ac.sem, P[("rr", itb)])
                if ("bcfree", itb % 2) in P:
                    t.wait_ge(dv.sem, P[("bcfree", itb % 2)])
                inst = t.matmul(pool[6 + itb % 2][:, :],
                                onesx_bf[0:1, 0:128], rrow[itb % 2][:, :],
                                start=True, stop=True)
                inst.then_inc(pe.sem, 1)
                P[("bcmm", itb)] = P[("rrfree", itb % 2)] = pe.add(1)

            @blk.vector
            def _(v, itb=itb, thb=thb, hb=hb):
                v.wait_ge(pe.sem, P[("bcmm", itb)])
                v.tensor_copy(bc_sb[itb % 2][:, :], pool[6 + itb % 2][:, :])
                P[("bcfree", itb % 2)] = n = vpub(v)
                v.wait_ge(dv.sem, n)
                v.tensor_mul(attn_fm[:, hb * T + thb * THW: hb * T + (thb + 1) * THW],
                             pool[2 + itb % 2][:, :], bc_sb[itb % 2][:, :])
                P[("qkvfree", itb % 2)] = vpub(v)

        for th in range(TH):
            for h in range(H):
                it = th * H + h

                @blk.tensor
                def _(t, th=th, h=h, it=it):
                    t.wait_ge(kbf.sem, P["kvbf"])
                    if it == 0:
                        t.wait_ge(pe.sem, e_end_pe)
                        t.wait_ge(dv.sem, e_end_dv)
                        t.wait_ge(ac.sem, ac.n)
                    if ("qkvfree", it % 2) in P:
                        t.wait_ge(dv.sem, P[("qkvfree", it % 2)])
                    if ("denfree", it % 2) in P:
                        t.wait_ge(ac.sem, P[("denfree", it % 2)])
                    mov = q_fm[:, h * T + th * THW: h * T + (th + 1) * THW]
                    t.matmul(pool[2 + it % 2][:, :], kv_bf[:, h * 129:h * 129 + 128],
                             mov, start=True, stop=True)
                    inst = t.matmul(pool[4 + it % 2][0:1, :],
                                    kv_bf[:, h * 129 + 128:h * 129 + 129],
                                    mov, start=True, stop=True)
                    inst.then_inc(pe.sem, 1)
                    P[("qkvmm", it)] = pe.add(1)

                @blk.scalar
                def _(sa, it=it):
                    sa.wait_ge(pe.sem, P[("qkvmm", it)])
                    if ("rrfree", it % 2) in P:
                        sa.wait_ge(pe.sem, P[("rrfree", it % 2)])
                    sa.activation(t1row[it % 2][:, :], pool[4 + it % 2][0:1, :],
                                  AF.Ln, bias=cst_sb[0:1, 0:1])
                    P[("denfree", it % 2)] = n = apub(sa)
                    sa.wait_ge(ac.sem, n)
                    sa.activation(rrow[it % 2][:, :], t1row[it % 2][:, :],
                                  AF.Exp, scale=cst_sb[0:1, 2:3])
                    P[("rr", it)] = apub(sa)

                if it >= 1:
                    emit_bc(it - 1)
                if it == TH * H - 1:
                    emit_bc(it)
        attn_ready = dv.n
        if DEBUG_DUMP:
            @blk.sync
            def _(s):
                s.wait_ge(dv.sem, attn_ready)
                s.dma_start(out=dbg["d_attn"], in_=attn_fm[:, :]).then_inc(do.sem, 16)
                if not SKIP_FFN:
                    s.wait_ge(ds.sem, P[("fsc", DC - 1, TH - 1)])
                    s.dma_start(out=dbg["d_ffn"], in_=scratch.ap()).then_inc(do.sem, 16)
                    do.add(16)
                do.add(16)
                s.wait_ge(do.sem, do.n)

        def emit_otr(jdx):
            jth, jdc = divmod(jdx, DC)

            @blk.tensor
            def _(t, jdx=jdx, jth=jth, jdc=jdc):
                t.wait_ge(dv.sem, P[("s1p", jdc, jth)])
                key = ("otrfree", jdx % 2)
                if key in P:
                    t.wait_ge(dv.sem, P[key])
                for q in range(4):
                    inst = t.transpose(pool[6 + jdx % 2][:, q * 128:(q + 1) * 128],
                                       s1[jdx % 2][:, q * 128:(q + 1) * 128],
                                       id_f32[:, :])
                inst.then_inc(pe.sem, 1)
                P[("otr", jdc, jth)] = P[("s1free", jdx % 2)] = pe.add(1)

            @blk.vector
            def _(v, jdx=jdx, jth=jth, jdc=jdc):
                v.wait_ge(pe.sem, P[("otr", jdc, jth)])
                if jth == 1 and jdc == 0:
                    v.wait_ge(osts[0].sem, 32)
                dst = ostage[:, :].rearrange("p (q d) -> p q d", d=D)[
                    :, :, jdc * 128:(jdc + 1) * 128]
                v.tensor_copy(dst, pool[6 + jdx % 2][:, :].rearrange(
                    "p (q n) -> p q n", n=128))
                P[("otrfree", jdx % 2)] = P[("oev", jdc, jth)] = vpub(v)

            if jdc % 8 == 7:
                kq = jdc // 8
                @blk.gpsimd
                def _(gp, jth=jth, kq=kq):
                    gp.wait_ge(dv.sem, P[("oev", 8 * kq + 7, jth)])
                    gp.wait_ge(pc.sem, 128)
                    dstq = out_h.rearrange("(q p) (kq n) -> p q kq n", p=128, n=1024)[
                        :, jth * 4:(jth + 1) * 4, kq, :]
                    srcq = ostage[:, :].rearrange("p (q kq n) -> p q kq n",
                                                  kq=2, n=1024)[:, :, kq, :]
                    gp.dma_start(out=dstq, in_=srcq,
                                 accum_op=ALU.add).then_inc(osts[jth].sem, 16)
                    P[("oacc", jth)] = osts[jth].add(16)

        for th in range(TH):
            for dc in range(DC):
                idx = th * DC + dc

                @blk.sync
                def _(s, th=th, dc=dc, idx=idx):
                    if idx == 0:
                        s.wait_ge(dv.sem, attn_ready)
                    if idx < 2:
                        s.wait_ge(pe.sem, e_end_pe)
                    if idx >= 2:
                        s.wait_ge(ac.sem, P[("wocast", idx - 2)])
                    s.dma_start(out=wocol32[idx % 2][:, :], in_=wo_v[dc]
                                ).then_inc(wos[idx % 2].sem, 16)
                    P[("wodma", dc, th)] = wos[idx % 2].add(16)

                @blk.scalar
                def _(sa, th=th, dc=dc, idx=idx):
                    sa.wait_ge(wos[idx % 2].sem, P[("wodma", dc, th)])
                    if idx >= 2:
                        pth, pdc = divmod(idx - 2, DC)
                        sa.wait_ge(pe.sem, P[("womm", pdc, pth)])
                    sa.activation(wocol[idx % 2][:, :], wocol32[idx % 2][:, :], AF.Identity)
                    P[("wocast", idx)] = apub(sa)

                @blk.sync
                def _(s, th=th, dc=dc, idx=idx):
                    fb = idx % 3
                    if idx < 3:
                        s.wait_ge(pe.sem, e_end_pe)
                    if not SKIP_FFN:
                        s.wait_ge(fws[dc % 2].sem, P[("fsc", dc, th)])
                    if idx >= 3:
                        pth, pdc = divmod(idx - 3, DC)
                        s.wait_ge(dv.sem, P[("s1p", pdc, pth)])
                    s.dma_start(out=fbuf[fb][:, :],
                                in_=scratch.ap()[dc * 128:(dc + 1) * 128,
                                                 th * THW:(th + 1) * THW]
                                ).then_inc(frs[fb].sem, 16)
                    P[("fread", dc, th)] = frs[fb].add(16)

                @blk.tensor
                def _(t, th=th, dc=dc, idx=idx):
                    t.wait_ge(ac.sem, P[("wocast", idx)])
                    if idx == 0:
                        t.wait_ge(dv.sem, attn_ready)
                    key = ("wofree", idx % 2)
                    if key in P:
                        t.wait_ge(dv.sem, P[key])
                    for c in range(17):
                        mov = (attn_fm[:, c * T + th * THW: c * T + (th + 1) * THW]
                               if c < 16 else onesx_bf[:, :])
                        inst = t.matmul(pool[idx % 2][:, :],
                                        wocol[idx % 2][:, c * 128:(c + 1) * 128],
                                        mov, start=(c == 0), stop=(c == 16))
                    inst.then_inc(pe.sem, 1)
                    P[("womm", dc, th)] = pe.add(1)

                @blk.vector
                def _(v, th=th, dc=dc, idx=idx):
                    v.wait_ge(pe.sem, P[("womm", dc, th)])
                    v.wait_ge(frs[idx % 3].sem, P[("fread", dc, th)])
                    key = ("s1free", idx % 2)
                    if key in P:
                        v.wait_ge(pe.sem, P[key])
                    v.tensor_add(s1[idx % 2][:, :], pool[idx % 2][:, :], fbuf[idx % 3][:, :])
                    P[("wofree", idx % 2)] = P[("s1p", dc, th)] = vpub(v)

                if idx >= 1:
                    emit_otr(idx - 1)
                if idx == TH * DC - 1:
                    emit_otr(idx)

        @blk.sync
        def _(s):
            for og in range(2):
                s.wait_ge(osts[og].sem, osts[og].n)

        stD.close()
        stQ.close()
    return nc


_CACHE = {}


def kernel(**inputs):
    global LAST_RES
    xs, shared = _prep(inputs)
    key = ("nc", SKIP_FFN, SKIP_ATTN, DEBUG_DUMP)
    if key not in _CACHE:
        _CACHE[key] = build_nc()
    nc = _CACHE[key]
    in_maps = []
    for c in range(NCORES):
        m = {"x": xs[c]}
        m.update(shared)
        in_maps.append(m)
    res = run_bass_kernel_spmd(nc, in_maps, core_ids=list(range(NCORES)))
    LAST_RES = res
    out = np.empty((2, 4 * T, D), np.float32)
    for c in range(NCORES):
        out[c // 4, (c % 4) * T:(c % 4 + 1) * T, :] = res.results[c]["out"]
    return out
